# version 02
# baseline (speedup 1.0000x reference)
import numpy as np
from contextlib import ExitStack

import concourse.bass as bass
import concourse.mybir as mybir
from concourse.bass_utils import run_bass_kernel_spmd

F32 = mybir.dt.float32
BF16 = mybir.dt.bfloat16
AF = mybir.ActivationFunctionType
ALU = mybir.AluOpType
AX = mybir.AxisListType

B, T, D = 2, 8192, 1024
NPAIR, E = 8, 8
NCORE = 8
PHASES = "CPOWAF"
DEBUG_TAPS = False
PGROUPS = 16
CSUB = 4
SHIFT = 32.0
NEGBIG = -30000.0
CAP = (64, 32)
GG = 4

ENGS = ("sync", "vector", "scalar", "tensor", "gpsimd")

WQ, WK, WV, LNF = 0, 8192, 16384, 24576
WO = 32896
QSL, KST, VST, AAB, ONORM = 0, 4096, 12288, 20608, 28800
XN, HHO, W1S, W2Q = 0, 16384, 24576, 26624
XB, JUNK = 0, 8192
WSTG = 9216
OACC, X1B, XOB = 0, 17408, 19456
SACC, CW, JUNKF, LNA = 0, 17408, 17536, 18560
LNT, KBUF, VBUF, QBUF = 0, 2048, 3072, 5152
WCST = 0
LN2N, CWB = 128, 0

COL_SHIFT, COL_EPS, COL_NHALF, COL_QSC = 96, 97, 98, 99


class Sched:
    def __init__(self):
        self.ops = {e: [] for e in ENGS}
        self.ctr = {s: 0 for s in ("qS", "qG", "qV", "tP", "tC", "tV", "tG")}
        self.last = {}

    def on(self, eng, fn):
        self.ops[eng].append(fn)

    def wait(self, eng, sem, val):
        if val is None or val <= 0:
            return
        self.on(eng, lambda E, s=sem, v=val: E.wait_ge(E._sems[s], v))

    def waitk(self, eng, key):
        tok = self.last.get(key)
        if tok is not None:
            self.wait(eng, tok[0], tok[1])

    def dma(self, eng, out, in_, sem):
        self.ctr[sem] += 16
        v = self.ctr[sem]
        self.on(eng, lambda E, o=out, i=in_, s=sem: E.dma_start(out=o, in_=i).then_inc(E._sems[s], 16))
        return v

    def op(self, eng, fn, sem=None):
        if sem is None:
            self.on(eng, fn)
            return None
        self.ctr[sem] += 1
        v = self.ctr[sem]
        def run(E, f=fn, s=sem):
            f(E).then_inc(E._sems[s], 1)
        self.on(eng, run)
        return v


def build_program():
    nc = bass.Bass("TRN2", target_bir_lowering=False, debug=False)
    ap = {}
    for name, shape in (("x", [B, T, D]), ("xown", [4, 512, D]),
                        ("wq", [16, D, 64]), ("wk", [16, D, 64]), ("wv", [16, D, 64]),
                        ("wo", [D, D]), ("gw", [D, E]),
                        ("w1", [E, D, 4096]), ("b1", [E, 4096]),
                        ("w2", [E, 4096, D]), ("b2", [E, D]),
                        ("gbc", [128, 16]), ("b1t", [128, 256]),
                        ("ident", [128, 128]), ("maskp", [4, 128, 512]),
                        ("btab", [128, 100])):
        ap[name] = nc.dram_tensor(name, shape, F32, kind="ExternalInput").ap()
    y = nc.dram_tensor("y", [4, 512, D], F32, kind="ExternalOutput").ap()
    taps = {}
    if DEBUG_TAPS:
        taps["t_kscr"] = nc.dram_tensor("t_kscr", [NPAIR, 128, T], BF16, kind="ExternalOutput").ap()
        taps["t_vscr"] = nc.dram_tensor("t_vscr", [T, 1040], BF16, kind="ExternalOutput").ap()
        taps["t_qscr"] = nc.dram_tensor("t_qscr", [4, 128, 4096], BF16, kind="ExternalOutput").ap()
        taps["t_kown"] = nc.dram_tensor("t_kown", [4, 128, 4096], BF16, kind="ExternalOutput").ap()
        taps["t_vown"] = nc.dram_tensor("t_vown", [4, 512, 1040], BF16, kind="ExternalOutput").ap()
        taps["t_x1"] = nc.dram_tensor("t_x1", [4, 512, D], F32, kind="ExternalOutput").ap()
        taps["t_w1bf"] = nc.dram_tensor("t_w1bf", [D, 4096], BF16, kind="ExternalOutput").ap()
        taps["t_st"] = nc.dram_tensor("t_st", [128, 64], F32, kind="ExternalOutput").ap()
        taps["t_ar3"] = nc.dram_tensor("t_ar3", [128, 8192], BF16, kind="ExternalOutput").ap()
        taps["t_ar2"] = nc.dram_tensor("t_ar2", [128, 9216], F32, kind="ExternalOutput").ap()

    scr = {
        "kscr": nc.dram_tensor("kscr", [B, NPAIR, 128, T], BF16, kind="Internal").ap(),
        "vscr": nc.dram_tensor("vscr", [B, T, 1040], BF16, kind="Internal").ap(),
        "qscr": nc.dram_tensor("qscr", [4, 128, 4096], BF16, kind="Internal").ap(),
        "kown": nc.dram_tensor("kown", [4, 128, 4096], BF16, kind="Internal").ap(),
        "vown": nc.dram_tensor("vown", [4, 512, 1040], BF16, kind="Internal").ap(),
        "x1scr": nc.dram_tensor("x1scr", [4, 512, D], F32, kind="Internal").ap(),
        "w1bf": nc.dram_tensor("w1bf", [E, D, 4096], BF16, kind="Internal").ap(),
        "w2bf": nc.dram_tensor("w2bf", [E, 4096, D], BF16, kind="Internal").ap(),
    }

    with ExitStack() as stk:
        En = stk.enter_context
        sb = {
            "AR1": En(nc.sbuf_tensor("AR1", [128, 43008], BF16)),
            "AR2": En(nc.sbuf_tensor("AR2", [128, 22528], F32)),
            "AR3": En(nc.sbuf_tensor("AR3", [128, 8192], BF16)),
            "c_ident": En(nc.sbuf_tensor("c_ident", [128, 128], BF16)),
            "c_maskp": En(nc.sbuf_tensor("c_maskp", [128, 2048], BF16)),
            "c_btab": En(nc.sbuf_tensor("c_btab", [128, 100], F32)),
            "c_gb": En(nc.sbuf_tensor("c_gb", [128, 16], F32)),
            "c_b1t": En(nc.sbuf_tensor("c_b1t", [128, 256], F32)),
            "c_b2": En(nc.sbuf_tensor("c_b2", [8, 1024], BF16)),
            "c_gw": En(nc.sbuf_tensor("c_gw", [128, 64], BF16)),
            "c_st": En(nc.sbuf_tensor("c_st", [128, 64], F32)),
            "c_on1": En(nc.sbuf_tensor("c_on1", [1, 64], F32)),
            "c_zr": En(nc.sbuf_tensor("c_zr", [1, 512], F32)),
            "c_zrh": En(nc.sbuf_tensor("c_zrh", [1, 1024], BF16)),
            "c_onb": En(nc.sbuf_tensor("c_onb", [1, 64], BF16)),
            "c_cwt": En(nc.sbuf_tensor("c_cwt", [8, 2048], BF16)),
            "c_f128": En(nc.sbuf_tensor("c_f128", [128, 128], F32)),
            "pA": En(nc.psum_tensor("pA", [128, 1024], F32)),
            "pB": En(nc.psum_tensor("pB", [128, 1024], F32)),
            "pC": En(nc.psum_tensor("pC", [128, 512], F32)),
            "pD": En(nc.psum_tensor("pD", [128, 512], F32)),
            "pT": En(nc.psum_tensor("pT", [128, 2048], BF16)),
        }
        sems = {s: En(nc.semaphore(s)) for s in ("qS", "qG", "qV", "tP", "tC", "tV", "tG")}

        S = Sched()
        emit_all(S, ap, y, scr, sb, taps)

        block = En(nc.Block())
        def section(eng_name):
            def body(E):
                E._sems = sems
                for fn in S.ops[eng_name]:
                    fn(E)
            return body
        block.sync(section("sync"))
        block.vector(section("vector"))
        block.scalar(section("scalar"))
        block.tensor(section("tensor"))
        block.gpsimd(section("gpsimd"))
    return nc


def emit_all(S, ap, y, scr, sb, taps=None):
    AR1, AR2, AR3 = sb["AR1"], sb["AR2"], sb["AR3"]
    pA, pB, pC, pD, pT = sb["pA"], sb["pB"], sb["pC"], sb["pD"], sb["pT"]
    c_btab, c_st, c_ident, c_maskp = sb["c_btab"], sb["c_st"], sb["c_ident"], sb["c_maskp"]
    last = S.last

    S.dma("sync", c_btab[:, :], ap["btab"], "qS")
    S.dma("sync", sb["c_f128"][:, :], ap["ident"], "qS")
    for mi in range(4):
        S.dma("sync", AR2[:, JUNK + mi * 512:JUNK + (mi + 1) * 512], ap["maskp"][mi], "qS")
    S.dma("sync", sb["c_gb"][:, :], ap["gbc"], "qS")
    S.dma("sync", sb["c_b1t"][:, :], ap["b1t"], "qS")
    S.dma("sync", AR2[0:8, JUNK + 3328:JUNK + 4352], ap["b2"], "qS")
    for c in range(8):
        S.dma("sync", AR2[:, JUNK + 4352 + c * 8:JUNK + 4352 + (c + 1) * 8],
              ap["gw"][c * 128:(c + 1) * 128, :], "qS")
    v_ld = S.ctr["qS"]

    S.wait("vector", "qS", v_ld)
    S.op("vector", lambda E: E.tensor_copy(c_ident[:, :], sb["c_f128"][:, :]))
    S.op("vector", lambda E: E.tensor_copy(c_maskp[:, :], AR2[:, JUNK:JUNK + 2048]))
    S.op("vector", lambda E: E.tensor_copy(sb["c_b2"][0:8, :], AR2[0:8, JUNK + 3328:JUNK + 4352]))
    S.op("vector", lambda E: E.tensor_copy(sb["c_gw"][:, :], AR2[:, JUNK + 4352:JUNK + 4416]))
    if CSUB >= 2:
        S.op("vector", lambda E: E.memset(sb["c_on1"][0:1, :], 1.0))
        S.op("vector", lambda E: E.memset(sb["c_onb"][0:1, :], 1.0))
    for vb in (range(2) if CSUB >= 2 else ()):
        off = VBUF + vb * 1040
        S.op("vector", lambda E, o=off: E.memset(
            AR3[:, o:o + 1040].rearrange("p (h a) -> p h a", a=65)[:, :, 64:65], 1.0))
    v_cast0 = S.op("vector", lambda E: E.engine_nop(), "tV")

    t_cT = S.ctr["tP"]
    t_cV = S.ctr["tV"]

    def stage_cast(n_dmas_fn, dst_off, ncols):
        n_dmas_fn()
        vq = S.ctr["qS"]
        S.wait("vector", "qS", vq)
        tv = S.op("vector", lambda E, o=dst_off, n=ncols: E.tensor_copy(
            AR1[:, o:o + n], AR2[:, XB:XB + n]), "tV")
        S.wait("sync", "tV", tv)
        return tv

    def pairs_loader(name):
        def go():
            for c in range(8):
                for pr in range(NPAIR):
                    blk = c * 8 + pr
                    src = ap[name][2 * pr:2 * pr + 2, c * 128:(c + 1) * 128, :] \
                        .rearrange("h p a -> p h a")
                    dst = AR2[:, XB + blk * 128:XB + (blk + 1) * 128] \
                        .rearrange("p (h a) -> p h a", a=64)
                    S.dma("sync", dst, src, "qS")
        return go
    if CSUB < 4:
        def stage_cast(a, b, c):
            return None
    stage_cast(pairs_loader("wq"), WQ, 8192)
    stage_cast(pairs_loader("wk"), WK, 8192)
    def wv_loader():
        for c in range(8):
            src = ap["wv"][:, c * 128:(c + 1) * 128, :].rearrange("h p a -> p h a")
            dst = AR2[:, XB + c * 1024:XB + (c + 1) * 1024] \
                .rearrange("p (h a) -> p h a", a=64)
            S.dma("sync", dst, src, "qS")
    stage_cast(wv_loader, WV, 8192)
    def wo_loader():
        for pr in range(NPAIR):
            S.dma("sync", AR2[:, XB + pr * 1024:XB + (pr + 1) * 1024],
                  ap["wo"][pr * 128:(pr + 1) * 128, :], "qS")
    stage_cast(wo_loader, WO, 8192)

    stat_slot = [0]

    def ln_block(src_ap, dst_ap, junk_off, junk2_off):
        sl = stat_slot[0] % 8
        stat_slot[0] += 1
        cm, cs2, cv2, crs = sl * 8, sl * 8 + 1, sl * 8 + 2, sl * 8 + 3
        tv0 = S.op("vector", lambda E: E.scalar_tensor_tensor(
            AR2[:, junk_off:junk_off + 1024], src_ap, 1.0, src_ap,
            ALU.mult, ALU.mult, accum_out=c_st[:, cs2:cs2 + 1]), "tV")
        tc1 = S.op("scalar", lambda E: E.activation(
            AR2[:, junk2_off:junk2_off + 1024], src_ap, AF.Copy,
            accum_out=c_st[:, cm:cm + 1]), "tC")
        S.wait("vector", "tC", tc1)
        S.wait("vector", "tV", tv0)
        t1 = S.op("vector", lambda E: E.tensor_scalar(
            c_st[:, cv2:cv2 + 1], c_st[:, cs2:cs2 + 1], 1.0 / 1024.0, None, ALU.mult), "tV")
        t2 = S.op("vector", lambda E: E.tensor_scalar(
            c_st[:, cm:cm + 1], c_st[:, cm:cm + 1], 1.0 / 1024.0, None, ALU.mult), "tV")
        S.wait("vector", "tV", t2)
        t3 = S.op("vector", lambda E: E.scalar_tensor_tensor(
            c_st[:, crs:crs + 1], c_st[:, cm:cm + 1], -1.0, c_st[:, cm:cm + 1],
            ALU.mult, ALU.mult), "tV")
        S.wait("vector", "tV", t3)
        t4 = S.op("vector", lambda E: E.tensor_add(
            c_st[:, cv2:cv2 + 1], c_st[:, cv2:cv2 + 1], c_st[:, crs:crs + 1]), "tV")
        S.wait("scalar", "tV", t4)
        t5 = S.op("scalar", lambda E: E.activation(
            c_st[:, crs:crs + 1], c_st[:, cv2:cv2 + 1], AF.Ln,
            bias=c_btab[:, COL_EPS:COL_EPS + 1]), "tC")
        S.wait("scalar", "tC", t5)
        tc2 = S.op("scalar", lambda E: E.activation(
            c_st[:, crs:crs + 1], c_st[:, crs:crs + 1], AF.Exp,
            scale=c_btab[:, COL_NHALF:COL_NHALF + 1]), "tC")
        S.wait("vector", "tC", tc2)
        tv5 = S.op("vector", lambda E: E.tensor_scalar(
            dst_ap, src_ap, c_st[:, cm:cm + 1], c_st[:, crs:crs + 1],
            ALU.subtract, ALU.mult), "tV")
        return ("tV", tv5)

    prst = {"grp": 0, "kq": 0}

    def proj_512(xsrc, kdst, vdst, qdst):
        gbuf = prst["grp"] % 2
        lnf_off = LNF + gbuf * 4096
        prst["grp"] += 1
        xoff = XB + gbuf * 4096
        S.waitk("sync", ("xb", gbuf))
        S.waitk("sync", ("xbc", gbuf))
        vx = S.dma("sync", AR2[:, xoff:xoff + 4096].rearrange("p (s d) -> p s d", d=1024),
                   xsrc.rearrange("(s p) d -> p s d", p=128), "qS")
        S.waitk("vector", ("lnfw", gbuf))
        for sub in range(4):
            lbuf = sub % 2
            loff = LNT + lbuf * 1024
            S.waitk("scalar", ("lnt", lbuf))
            S.waitk("vector", ("lnt", lbuf))
            S.wait("scalar", "qS", vx)
            S.wait("vector", "qS", vx)
            S.wait("vector", "tP", t_cT)
            tln = ln_block(AR2[:, xoff + sub * 1024:xoff + (sub + 1) * 1024],
                           AR3[:, loff:loff + 1024], JUNK, JUNK + 1024)
            ptb = sub % 2
            pto = ptb * 1024
            S.waitk("tensor", ("pT", ptb))
            S.wait("tensor", tln[0], tln[1])
            tp = None
            for c in range(8):
                tp = S.op("tensor", lambda E, c=c, o=loff, po=pto: E.transpose(
                    pT[:, po + c * 128:po + (c + 1) * 128],
                    AR3[:, o + c * 128:o + (c + 1) * 128], c_ident[:, :]),
                    "tP" if c == 7 else None)
            last[("lnt", lbuf)] = ("tP", tp)
            S.wait("vector", "tP", tp)
            tv = S.op("vector", lambda E, po=pto, s=sub, lo=lnf_off: E.tensor_copy(
                AR1[:, lo:lo + 4096].rearrange("p (c n) -> p c n", n=512)[:, :, s * 128:(s + 1) * 128],
                pT[:, po:po + 1024].rearrange("p (c n) -> p c n", n=128)), "tV")
            last[("pT", ptb)] = ("tV", tv)
        v_lnf = S.ctr["tV"]
        last[("xb", gbuf)] = ("tV", S.ctr["tV"])
        last[("xbc", gbuf)] = ("tC", S.ctr["tC"])

        def pair_proj(w_off, dsts, scale_col, tag):
            for pr in range(NPAIR):
                pp = prst["kq"] % 2
                kb = prst["kq"] % 2
                prst["kq"] += 1
                ps = pC if pp == 0 else pD
                S.waitk("tensor", ("pKQ", pp))
                S.wait("tensor", "tV", v_lnf)
                tp = None
                for c in range(8):
                    tp = S.op("tensor", lambda E, c=c, pr=pr, ps=ps, wo=w_off, lo=lnf_off: E.matmul(
                        ps[:, :], AR1[:, wo + (c * 8 + pr) * 128:wo + (c * 8 + pr + 1) * 128],
                        AR1[:, lo + c * 512:lo + (c + 1) * 512],
                        start=(c == 0), stop=(c == 7)), "tP" if c == 7 else None)
                koff = (KBUF if tag == "k" else QBUF) + kb * 512
                if scale_col is None:
                    S.waitk("scalar", (tag + "buf", kb))
                    S.wait("scalar", "tP", tp)
                else:
                    S.waitk("vector", (tag + "buf", kb))
                    S.wait("vector", "tP", tp)
                if scale_col is None:
                    tc = S.op("scalar", lambda E, ps=ps, ko=koff: E.activation(
                        AR3[:, ko:ko + 512], ps[:, :], AF.Copy), "tC")
                    last[("pKQ", pp)] = ("tC", tc)
                    S.wait("gpsimd", "tC", tc)
                else:
                    tc = S.op("vector", lambda E, ps=ps, ko=koff: E.tensor_scalar(
                        AR3[:, ko:ko + 512], ps[:, :], 0.125, None, ALU.mult), "tV")
                    last[("pKQ", pp)] = ("tV", tc)
                    S.wait("gpsimd", "tV", tc)
                vg = S.dma("gpsimd", dsts[pr], AR3[:, koff:koff + 512], "qG")
                last[(tag + "buf", kb)] = ("qG", vg)

        pair_proj(WK, kdst, None, "k")
        if qdst is not None:
            pair_proj(WQ, qdst, COL_QSC, "q")

        for sub in range(4):
            pp = sub % 2
            psv = pA if pp == 0 else pB
            S.waitk("tensor", ("pV", pp))
            S.wait("tensor", "tV", v_lnf)
            tp = None
            for dh in range(2):
                for c in range(8):
                    tp = S.op("tensor", lambda E, c=c, dh=dh, s=sub, psv=psv, lo=lnf_off: E.matmul(
                        psv[:, dh * 512:(dh + 1) * 512],
                        AR1[:, lo + c * 512 + s * 128:lo + c * 512 + (s + 1) * 128],
                        AR1[:, WV + c * 1024 + dh * 512:WV + c * 1024 + (dh + 1) * 512],
                        start=(c == 0), stop=(c == 7)), "tP" if c == 7 else None)
            vb = sub % 2
            voff = VBUF + vb * 1040
            S.waitk("scalar", ("vbuf", vb))
            S.wait("scalar", "tP", tp)
            tc = None
            for dh in range(2):
                tc = S.op("scalar", lambda E, dh=dh, vo=voff, psv=psv: E.activation(
                    AR3[:, vo + dh * 520:vo + (dh + 1) * 520]
                    .rearrange("p (h a) -> p h a", a=65)[:, :, 0:64],
                    psv[:, dh * 512:(dh + 1) * 512].rearrange("p (h a) -> p h a", a=64),
                    AF.Copy), "tC")
            last[("pV", pp)] = ("tC", tc)
            S.wait("gpsimd", "tC", tc)
            vg = S.dma("gpsimd", vdst[sub], AR3[:, voff:voff + 1040], "qG")
            last[("vbuf", vb)] = ("qG", vg)
        last[("lnfw", gbuf)] = ("tP", S.ctr["tP"])

    for b in (range(B) if "P" in PHASES else ()):
        for g in range(PGROUPS):
            proj_512(ap["x"][b, g * 512:(g + 1) * 512, :],
                     [scr["kscr"][b, pr, :, g * 512:(g + 1) * 512] for pr in range(NPAIR)],
                     [scr["vscr"][b, g * 512 + s * 128:g * 512 + (s + 1) * 128, :] for s in range(4)],
                     None)
    for gs in (range(4) if "O" in PHASES else ()):
        proj_512(ap["xown"][gs],
                 [scr["kown"][gs, :, pr * 512:(pr + 1) * 512] for pr in range(NPAIR)],
                 [scr["vown"][gs, s * 128:(s + 1) * 128, :] for s in range(4)],
                 [scr["qscr"][gs, :, pr * 512:(pr + 1) * 512] for pr in range(NPAIR)])
    t_PO_P = S.ctr["tP"]
    t_PO_C = S.ctr["tC"]
    v_PO_G = S.ctr["qG"]

    S.wait("gpsimd", "qG", v_PO_G)
    wblk = [0]

    def precast(src, dst):
        wb = wblk[0] % 2
        wblk[0] += 1
        S.waitk("gpsimd", ("wstg", wb))
        dst_st = AR2[:, WSTG + wb * 4096:WSTG + (wb + 1) * 4096]
        if len(src.shape) == 3:
            dst_st = dst_st.rearrange("p (s d) -> p s d", d=src.shape[2])
        vq = S.dma("gpsimd", dst_st, src, "qG")
        S.waitk("gpsimd", ("wcst", wb))
        S.wait("gpsimd", "qG", vq)
        tg = S.op("gpsimd", lambda E, wb=wb: E.tensor_copy(
            AR3[:, WCST + wb * 4096:WCST + (wb + 1) * 4096],
            AR2[:, WSTG + wb * 4096:WSTG + (wb + 1) * 4096]), "tG")
        last[("wstg", wb)] = ("tG", tg)
        src_st = AR3[:, WCST + wb * 4096:WCST + (wb + 1) * 4096]
        if len(dst.shape) == 3:
            src_st = src_st.rearrange("p (s d) -> p s d", d=dst.shape[2])
        vg = S.dma("gpsimd", dst, src_st, "qG")
        last[("wcst", wb)] = ("qG", vg)

    for e in (range(E) if "W" in PHASES else ()):
        for blk in range(8):
            precast(ap["w1"][e, blk * 128:(blk + 1) * 128, :],
                    scr["w1bf"][e, blk * 128:(blk + 1) * 128, :])
        for blk in range(8):
            precast(ap["w2"][e, blk * 512:(blk + 1) * 512, :].rearrange("(s p) d -> p s d", p=128),
                    scr["w2bf"][e, blk * 512:(blk + 1) * 512, :].rearrange("(s p) d -> p s d", p=128))
    v_W_G = S.ctr["qG"]
    t_W_G = S.ctr["tG"]

    for k in (("qsl",), ("kst", 0), ("kst", 1), ("spsV", 0), ("spsV", 1)):
        last[k] = ("tP", t_PO_P)
    sscnt = [0]
    aabc = [0]
    ofirst = [True] * 16

    def attn_group(s, koff, voff, grp, diag):
        for pr in range(NPAIR):
            ab = aabc[0] % 2
            aabc[0] += 1
            aoff = AAB + ab * 4096
            S.waitk("scalar", ("aab", ab))
            texp = []
            for i in range(GG):
                sslot = sscnt[0] % 2
                sscnt[0] += 1
                ps = pA if sslot == 0 else pB
                S.waitk("tensor", ("sps", sslot))
                S.waitk("tensor", ("spsV", sslot))
                for h in range(2):
                    inc = (h == 1) and not diag
                    S.op("tensor", lambda E, h=h, i=i, pr=pr, ps=ps, ko=koff: E.matmul(
                        ps[:, h * 512:(h + 1) * 512],
                        AR1[h * 64:(h + 1) * 64, ko + pr * 512 + i * 128:ko + pr * 512 + (i + 1) * 128],
                        AR1[h * 64:(h + 1) * 64, QSL + pr * 512:QSL + (pr + 1) * 512],
                        start=True, stop=(not diag), tile_position=(h * 64, 0)),
                        "tP" if inc else None)
                    if diag:
                        S.op("tensor", lambda E, h=h, i=i, ps=ps: E.matmul(
                            ps[:, h * 512:(h + 1) * 512], c_ident[:, :],
                            c_maskp[:, i * 512:(i + 1) * 512],
                            start=False, stop=True), "tP" if h == 1 else None)
                tp = S.ctr["tP"]
                S.wait("scalar", "tP", tp)
                bcol = 96 if diag else (grp * GG + i if s == 0 else 64 + grp * GG + i)
                tc = S.op("scalar", lambda E, ps=ps, i=i, ao=aoff, bc=bcol: E.activation(
                    AR1[:, ao + i * 1024:ao + (i + 1) * 1024], ps[:, :], AF.Exp,
                    bias=c_btab[:, bc:bc + 1]), "tC")
                texp.append(tc)
                last[("sps", sslot)] = ("tC", tc)
            tp_av = None
            for h in range(2):
                psz = pC if h == 0 else pD
                S.waitk("tensor", ("avz", h))
                S.wait("tensor", "tC", texp[-1])
                for i in range(GG):
                    tp_av = S.op("tensor", lambda E, i=i, h=h, pr=pr, psz=psz, vo=voff, ao=aoff: E.matmul(
                        psz[0:65, :],
                        AR1[:, vo + i * 1040 + (2 * pr + h) * 65:vo + i * 1040 + (2 * pr + h) * 65 + 65],
                        AR1[:, ao + i * 1024 + h * 512:ao + i * 1024 + (h + 1) * 512],
                        start=(i == 0), stop=(i == GG - 1)), "tP" if i == GG - 1 else None)
                slot = pr * 2 + h
                ooff = OACC + slot * 512
                S.wait("vector", "tP", tp_av)
                if ofirst[slot]:
                    tv = S.op("vector", lambda E, o=ooff, psz=psz: E.tensor_copy(
                        AR2[0:65, o:o + 512], psz[0:65, :]), "tV")
                    ofirst[slot] = False
                else:
                    tv = S.op("vector", lambda E, o=ooff, psz=psz: E.tensor_add(
                        AR2[0:65, o:o + 512], AR2[0:65, o:o + 512], psz[0:65, :]), "tV")
                last[("avz", h)] = ("tV", tv)
            last[("aab", ab)] = ("tP", tp_av)

    for bs in (range(4) if "A" in PHASES else ()):
        b, s = bs // 2, bs % 2
        S.waitk("sync", ("qsl",))
        vq = S.dma("sync", AR1[:, QSL:QSL + 4096], scr["qscr"][bs], "qS")
        S.wait("sync", "qG", v_PO_G)
        S.wait("tensor", "qS", vq)
        for i in range(16):
            ofirst[i] = True
        ngrp = CAP[s] // GG
        for grp in range(ngrp):
            kb = grp % 2
            S.waitk("sync", ("kst", kb))
            S.dma("sync", AR1[:, KST + kb * 4096:KST + (kb + 1) * 4096]
                  .rearrange("p (r t) -> p r t", t=512),
                  scr["kscr"][b, :, :, grp * 512:(grp + 1) * 512]
                  .rearrange("r p t -> p r t"), "qS")
            vv = S.dma("sync", AR1[:, VST + kb * 4160:VST + (kb + 1) * 4160]
                       .rearrange("p (i d) -> p i d", d=1040),
                       scr["vscr"][b, grp * 512:(grp + 1) * 512, :]
                       .rearrange("(i p) d -> p i d", p=128), "qS")
            S.wait("tensor", "qS", vv)
            attn_group(s, KST + kb * 4096, VST + kb * 4160, grp, False)
            last[("kst", kb)] = ("tP", S.ctr["tP"])
        S.waitk("sync", ("kst", 0))
        S.dma("sync", AR1[:, KST:KST + 4096], scr["kown"][bs], "qS")
        vv = S.dma("sync", AR1[:, VST:VST + 4160].rearrange("p (i d) -> p i d", d=1040),
                   scr["vown"][bs].rearrange("(i p) d -> p i d", p=128), "qS")
        S.wait("tensor", "qS", vv)
        attn_group(s, KST, VST, 0, True)
        last[("kst", 0)] = ("tP", S.ctr["tP"])

        for pr in range(NPAIR):
            for h in range(2):
                slot = pr * 2 + h
                ooff = OACC + slot * 512
                S.waitk("vector", ("zrw",))
                z1 = S.op("vector", lambda E, o=ooff: E.reciprocal(
                    sb["c_zr"][0:1, :], AR2[64:65, o:o + 512]), "tV")
                S.wait("vector", "tV", z1)
                z2 = S.op("vector", lambda E: E.tensor_copy(
                    sb["c_zrh"][0:1, 0:512], sb["c_zr"][0:1, :]), "tV")
                S.wait("vector", "tV", z2)
                z3 = S.op("vector", lambda E: E.tensor_sub(
                    sb["c_zr"][0:1, :], sb["c_zr"][0:1, :],
                    sb["c_zrh"][0:1, 0:512]), "tV")
                S.wait("vector", "tV", z3)
                tvr = S.op("vector", lambda E: E.tensor_copy(
                    sb["c_zrh"][0:1, 512:1024], sb["c_zr"][0:1, :]), "tV")
                S.waitk("tensor", ("bcast",))
                S.wait("tensor", "tV", tvr)
                tpb = S.op("tensor", lambda E: E.matmul(
                    pC[0:64, :], sb["c_onb"][0:1, :], sb["c_zrh"][0:1, 0:512],
                    start=True, stop=False), None)
                tpb = S.op("tensor", lambda E: E.matmul(
                    pC[0:64, :], sb["c_onb"][0:1, :], sb["c_zrh"][0:1, 512:1024],
                    start=False, stop=True), "tP")
                last[("zrw",)] = ("tP", tpb)
                S.wait("vector", "tP", tpb)
                tvn = S.op("vector", lambda E, o=ooff, h=h, pr=pr: E.tensor_mul(
                    AR1[h * 64:(h + 1) * 64, ONORM + pr * 512:ONORM + (pr + 1) * 512],
                    AR2[0:64, o:o + 512], pC[0:64, :]), "tV")
                last[("bcast",)] = ("tV", tvn)
        v_on = S.ctr["tV"]
        last[("avz", 0)] = ("tV", v_on)

        for sub in range(4):
            xb2 = sub % 2
            S.waitk("sync", ("xob", xb2))
            vx = S.dma("sync", AR2[:, XOB + xb2 * 1024:XOB + (xb2 + 1) * 1024],
                       ap["xown"][bs, sub * 128:(sub + 1) * 128, :], "qS")
            pw = sub % 2
            psw = pA if pw == 0 else pB
            S.waitk("tensor", ("sps", pw))
            S.waitk("tensor", ("spsV", pw))
            S.wait("tensor", "tV", v_on)
            tp = None
            for dh in range(2):
                for pr in range(NPAIR):
                    tp = S.op("tensor", lambda E, dh=dh, pr=pr, sub=sub, psw=psw: E.matmul(
                        psw[:, dh * 512:(dh + 1) * 512],
                        AR1[:, ONORM + pr * 512 + sub * 128:ONORM + pr * 512 + (sub + 1) * 128],
                        AR1[:, WO + pr * 1024 + dh * 512:WO + pr * 1024 + (dh + 1) * 512],
                        start=(pr == 0), stop=(pr == 7)), "tP" if pr == 7 else None)
            x1b = sub % 2
            S.waitk("vector", ("x1b", x1b))
            S.wait("vector", "tP", tp)
            S.wait("vector", "qS", vx)
            tv = None
            for dh in range(2):
                tv = S.op("vector", lambda E, dh=dh, x1b=x1b, xb2=xb2, psw=psw: E.tensor_add(
                    AR2[:, X1B + x1b * 1024 + dh * 512:X1B + x1b * 1024 + (dh + 1) * 512],
                    psw[:, dh * 512:(dh + 1) * 512],
                    AR2[:, XOB + xb2 * 1024 + dh * 512:XOB + xb2 * 1024 + (dh + 1) * 512]), "tV")
            last[("spsV", pw)] = ("tV", tv)
            last[("sps", pw)] = ("tV", tv)
            last[("xob", xb2)] = ("tV", tv)
            S.wait("gpsimd", "tV", tv)
            vg = S.dma("gpsimd", scr["x1scr"][bs, sub * 128:(sub + 1) * 128, :],
                       AR2[:, X1B + x1b * 1024:X1B + (x1b + 1) * 1024], "qG")
            last[("x1b", x1b)] = ("qG", vg)
        last[("qsl",)] = ("tP", S.ctr["tP"])
    t_A_P = S.ctr["tP"]
    t_A_V = S.ctr["tV"]
    v_A_G = S.ctr["qG"]

    S.wait("sync", "tV", t_A_V)
    S.wait("sync", "qG", v_A_G)
    S.wait("sync", "tG", t_W_G)
    S.wait("scalar", "qG", v_W_G)
    S.wait("scalar", "tP", t_A_P)
    S.wait("vector", "tP", t_A_P)
    for k in (("pgate",), ("pcwt",), ("pb2",)):
        last[k] = ("tV", t_A_V)

    for g in (range(4) if "F" in PHASES else ()):
        vx = S.dma("sync", AR2[:, SACC + g * 4096:SACC + (g + 1) * 4096]
                   .rearrange("p (s d) -> p s d", d=1024),
                   scr["x1scr"][g].rearrange("(s p) d -> p s d", p=128), "qS")
        for sub in range(4):
            soff = SACC + g * 4096 + sub * 1024
            S.wait("scalar", "qS", vx)
            S.wait("vector", "qS", vx)
            S.waitk("scalar", ("lnaw",))
            S.waitk("vector", ("lnaw",))
            tln1 = ln_block(AR2[:, soff:soff + 1024], AR2[:, LNA:LNA + 1024], JUNKF, 19584)
            S.wait("vector", tln1[0], tln1[1])
            tva = S.op("vector", lambda E, so=soff: E.tensor_add(
                AR2[:, so:so + 1024], AR2[:, so:so + 1024], AR2[:, LNA:LNA + 1024]), "tV")
            last[("lnaw",)] = ("tV", tva)
            S.wait("scalar", "tV", tva)
            S.wait("vector", "tV", tva)
            tln2 = ln_block(AR2[:, LNA:LNA + 1024], AR3[:, LN2N:LN2N + 1024], JUNKF, 19584)
            S.waitk("tensor", ("pTf",))
            S.wait("tensor", tln2[0], tln2[1])
            tp = None
            for c in range(8):
                tp = S.op("tensor", lambda E, c=c: E.transpose(
                    pT[:, c * 128:(c + 1) * 128],
                    AR3[:, LN2N + c * 128:LN2N + (c + 1) * 128], c_ident[:, :]),
                    "tP" if c == 7 else None)
            S.wait("vector", "tP", tp)
            tcx = None
            for c in range(8):
                tcx = S.op("vector", lambda E, c=c, g=g, sub=sub: E.tensor_scalar(
                    AR1[:, XN + c * 2048 + g * 512 + sub * 128:XN + c * 2048 + g * 512 + (sub + 1) * 128],
                    pT[:, c * 128:(c + 1) * 128],
                    sb["c_gb"][:, c:c + 1], sb["c_gb"][:, 8 + c:9 + c],
                    ALU.mult, ALU.add), "tV")
            last[("pTf",)] = ("tV", tcx)
            S.waitk("tensor", ("pgate",))
            S.wait("tensor", "tV", tcx)
            tpg = None
            for c in range(8):
                tpg = S.op("tensor", lambda E, c=c, g=g, sub=sub: E.matmul(
                    pC[:, 0:8],
                    AR1[:, XN + c * 2048 + g * 512 + sub * 128:XN + c * 2048 + g * 512 + (sub + 1) * 128],
                    sb["c_gw"][:, c * 8:(c + 1) * 8],
                    start=(c == 0), stop=(c == 7)), "tP" if c == 7 else None)
            cwo = CW + (g * 4 + sub) * 8
            sl = stat_slot[0] % 8
            stat_slot[0] += 1
            cm1, cm1n, cm2, cs2 = sl * 8, sl * 8 + 1, sl * 8 + 2, sl * 8 + 3
            S.wait("vector", "tP", tpg)
            w1_ = S.op("vector", lambda E, cm1=cm1: E.reduce_max(
                c_st[:, cm1:cm1 + 1], pC[:, 0:8], axis=AX.X), "tV")
            S.wait("vector", "tV", w1_)
            w2_ = S.op("vector", lambda E, cm1n=cm1n, cm1=cm1: E.tensor_scalar(
                c_st[:, cm1n:cm1n + 1], c_st[:, cm1:cm1 + 1], -1.0, None, ALU.mult), "tV")
            w3_ = S.op("vector", lambda E: E.tensor_copy(
                AR2[:, JUNKF:JUNKF + 8], pC[:, 0:8]), "tV")
            S.wait("vector", "tV", w3_)
            w4_ = S.op("vector", lambda E, cm1=cm1: E.tensor_scalar(
                AR2[:, JUNKF + 8:JUNKF + 16], AR2[:, JUNKF:JUNKF + 8],
                c_st[:, cm1:cm1 + 1], None, ALU.is_equal), "tV")
            S.wait("vector", "tV", w4_)
            w5_ = S.op("vector", lambda E: E.scalar_tensor_tensor(
                AR2[:, JUNKF + 8:JUNKF + 16], AR2[:, JUNKF + 8:JUNKF + 16], -1e30,
                AR2[:, JUNKF:JUNKF + 8], ALU.mult, ALU.add), "tV")
            S.wait("vector", "tV", w5_)
            w6_ = S.op("vector", lambda E, cm2=cm2: E.reduce_max(
                c_st[:, cm2:cm2 + 1], AR2[:, JUNKF + 8:JUNKF + 16], axis=AX.X), "tV")
            S.wait("vector", "tV", w6_)
            tvm = S.op("vector", lambda E, cm2=cm2: E.tensor_scalar(
                AR2[:, JUNKF + 8:JUNKF + 16], AR2[:, JUNKF:JUNKF + 8],
                c_st[:, cm2:cm2 + 1], None, ALU.is_ge), "tV")
            S.wait("scalar", "tV", max(tvm, w2_))
            tce = S.op("scalar", lambda E, cm1n=cm1n: E.activation(
                AR2[:, JUNKF + 16:JUNKF + 24], AR2[:, JUNKF:JUNKF + 8], AF.Exp,
                bias=c_st[:, cm1n:cm1n + 1]), "tC")
            S.wait("vector", "tC", tce)
            w7_ = S.op("vector", lambda E: E.tensor_mul(
                AR2[:, JUNKF + 16:JUNKF + 24], AR2[:, JUNKF + 16:JUNKF + 24],
                AR2[:, JUNKF + 8:JUNKF + 16]), "tV")
            S.wait("vector", "tV", w7_)
            w8_ = S.op("vector", lambda E, cs2=cs2: E.reduce_sum(
                c_st[:, cs2:cs2 + 1], AR2[:, JUNKF + 16:JUNKF + 24], axis=AX.X), "tV")
            S.wait("vector", "tV", w8_)
            w9_ = S.op("vector", lambda E, cs2=cs2: E.reciprocal(
                c_st[:, cs2 + 1:cs2 + 2], c_st[:, cs2:cs2 + 1]), "tV")
            S.wait("vector", "tV", w9_)
            tvc = S.op("vector", lambda E, cwo=cwo, cs2=cs2: E.tensor_scalar(
                AR2[:, cwo:cwo + 8], AR2[:, JUNKF + 16:JUNKF + 24],
                c_st[:, cs2 + 1:cs2 + 2], None, ALU.mult), "tV")
            last[("pgate",)] = ("tV", tvc)
            tvb = S.op("vector", lambda E, cwo=cwo: E.tensor_copy(
                AR3[:, CWB + 8:CWB + 16] if False else AR3[:, 0:8], AR2[:, cwo:cwo + 8]), "tV")
            S.waitk("tensor", ("pcwt",))
            S.wait("tensor", "tV", tvb)
            tpt = S.op("tensor", lambda E: E.transpose(
                pT[0:8, 1024:1152], AR3[:, 0:8], c_ident[:, :]), "tP")
            S.wait("vector", "tP", tpt)
            tvt = S.op("vector", lambda E, g=g, sub=sub: E.tensor_copy(
                sb["c_cwt"][0:8, g * 512 + sub * 128:g * 512 + (sub + 1) * 128],
                pT[0:8, 1024:1152]), "tV")
            last[("pcwt",)] = ("tV", tvt)
            S.waitk("tensor", ("pb2",))
            S.wait("tensor", "tV", tvt)
            tpb = None
            for dh in range(2):
                tpb = S.op("tensor", lambda E, dh=dh, g=g, sub=sub: E.matmul(
                    pC[:, :] if dh == 0 else pD[:, :],
                    sb["c_cwt"][0:8, g * 512 + sub * 128:g * 512 + (sub + 1) * 128],
                    sb["c_b2"][0:8, dh * 512:(dh + 1) * 512],
                    start=True, stop=True), "tP")
            S.wait("vector", "tP", tpb)
            tvb2 = None
            for dh in range(2):
                psb = pC if dh == 0 else pD
                tvb2 = S.op("vector", lambda E, dh=dh, so=soff, psb=psb: E.tensor_add(
                    AR2[:, so + dh * 512:so + (dh + 1) * 512],
                    AR2[:, so + dh * 512:so + (dh + 1) * 512], psb[:, :]), "tV")
            last[("pb2",)] = ("tV", tvb2)
    t_prep_C = S.ctr["tC"]
    t_prep_V = S.ctr["tV"]

    for k in (("ps1", 0), ("ps1", 1)):
        last[k] = ("tV", t_prep_V)
    for k in (("w1s", 0), ("w1s", 1), ("w2q", 0), ("w2q", 1)):
        last[k] = ("tP", t_A_P)
    s1c = [0]
    s2c = [0]
    for e in (range(E) if "F" in PHASES else ()):
        for g in range(4):
            for hh in range(2):
                for hcl in range(16):
                    hc = hh * 16 + hcl
                    wb = s1c[0] % 2
                    S.waitk("scalar", ("w1s", wb))
                    vq = S.dma("scalar", AR1[:, W1S + wb * 1024:W1S + (wb + 1) * 1024]
                               .rearrange("p (c n) -> p c n", n=128),
                               scr["w1bf"][e, :, hc * 128:(hc + 1) * 128]
                               .rearrange("(c p) n -> p c n", p=128), "qV")
                    pp = s1c[0] % 2
                    s1c[0] += 1
                    ps1 = pC if pp == 0 else pD
                    S.waitk("tensor", ("ps1", pp))
                    S.wait("tensor", "qV", vq)
                    S.wait("tensor", "tV", t_prep_V)
                    tp = None
                    for c in range(8):
                        tp = S.op("tensor", lambda E, c=c, wb=wb, g=g, ps1=ps1: E.matmul(
                            ps1[:, :],
                            AR1[:, W1S + wb * 1024 + c * 128:W1S + wb * 1024 + (c + 1) * 128],
                            AR1[:, XN + c * 2048 + g * 512:XN + c * 2048 + (g + 1) * 512],
                            start=(c == 0), stop=(c == 7)), "tP" if c == 7 else None)
                    last[("w1s", wb)] = ("tP", tp)
                    S.waitk("scalar", ("hh", hcl))
                    S.wait("scalar", "tP", tp)
                    tc = S.op("scalar", lambda E, hcl=hcl, hc=hc, e=e, ps1=ps1: E.activation(
                        AR1[:, HHO + hcl * 512:HHO + (hcl + 1) * 512], ps1[:, :], AF.Gelu,
                        bias=sb["c_b1t"][:, e * 32 + hc:e * 32 + hc + 1]), "tC")
                    last[("ps1", pp)] = ("tC", tc)
                t_gel = S.ctr["tC"]
                for dh in range(2):
                    wb2 = s2c[0] % 2
                    S.waitk("sync", ("w2q", wb2))
                    vq2 = S.dma("sync", AR1[:, W2Q + wb2 * 8192:W2Q + (wb2 + 1) * 8192]
                                .rearrange("p (k n) -> p k n", n=512),
                                scr["w2bf"][e, hh * 2048:(hh + 1) * 2048, dh * 512:(dh + 1) * 512]
                                .rearrange("(k p) n -> p k n", p=128), "qS")
                    for sub in range(4):
                        pp2 = s2c[0] % 2
                        s2c[0] += 1
                        ps2 = pA if pp2 == 0 else pB
                        S.waitk("tensor", ("ps2", pp2))
                        S.wait("tensor", "qS", vq2)
                        S.wait("tensor", "tC", t_gel)
                        tp = None
                        for kk in range(16):
                            tp = S.op("tensor", lambda E, kk=kk, sub=sub, wb2=wb2, ps2=ps2: E.matmul(
                                ps2[:, 0:512],
                                AR1[:, HHO + kk * 512 + sub * 128:HHO + kk * 512 + (sub + 1) * 128],
                                AR1[:, W2Q + wb2 * 8192 + kk * 512:W2Q + wb2 * 8192 + (kk + 1) * 512],
                                start=(kk == 0), stop=(kk == 15)), "tP" if kk == 15 else None)
                        last[("w2q", wb2)] = ("tP", tp)
                        soff = SACC + g * 4096 + sub * 1024 + dh * 512
                        cwo = CW + (g * 4 + sub) * 8 + e
                        S.wait("vector", "tP", tp)
                        tv = S.op("vector", lambda E, so=soff, cwo=cwo, ps2=ps2: E.scalar_tensor_tensor(
                            AR2[:, so:so + 512], ps2[:, 0:512], AR2[:, cwo:cwo + 1],
                            AR2[:, so:so + 512], ALU.mult, ALU.add), "tV")
                        last[("ps2", pp2)] = ("tV", tv)
                for hcl in range(16):
                    last[("hh", hcl)] = ("tP", S.ctr["tP"])
    t_F_V = S.ctr["tV"]

    if taps:
        S.wait("gpsimd", "tV", t_F_V)
        S.wait("gpsimd", "tP", S.ctr["tP"])
        S.dma("gpsimd", taps["t_kscr"], scr["kscr"][0], "qG")
        S.dma("gpsimd", taps["t_vscr"], scr["vscr"][0], "qG")
        S.dma("gpsimd", taps["t_qscr"], scr["qscr"][:, :, :], "qG")
        S.dma("gpsimd", taps["t_kown"], scr["kown"][:, :, :], "qG")
        S.dma("gpsimd", taps["t_vown"], scr["vown"][:, :, :], "qG")
        S.dma("gpsimd", taps["t_x1"], scr["x1scr"][:, :, :], "qG")
        S.dma("gpsimd", taps["t_w1bf"], scr["w1bf"][0], "qG")
        S.dma("gpsimd", taps["t_st"], sb["c_st"][:, :], "qG")
        S.dma("gpsimd", taps["t_ar3"], sb["AR3"][:, :], "qG")
        S.dma("gpsimd", taps["t_ar2"], sb["AR2"][:, 0:9216], "qG")

    S.wait("gpsimd", "tV", t_F_V)
    for g in (range(4) if "F" in PHASES else ()):
        S.dma("gpsimd", y[g].rearrange("(s p) d -> p s d", p=128),
              AR2[:, SACC + g * 4096:SACC + (g + 1) * 4096]
              .rearrange("p (s d) -> p s d", d=1024), "qG")
    S.wait("gpsimd", "qG", S.ctr["qG"])


def _host_constants(j):
    ident = np.eye(128, dtype=np.float32)
    maskp = np.zeros((4, 128, 512), np.float32)
    for i in range(4):
        kt = i * 128 + np.arange(128)[:, None]
        q = np.arange(512)[None, :]
        maskp[i] = np.where(kt <= q, 0.0, NEGBIG)
    btab = np.zeros((128, 100), np.float32)
    len0 = 4 * (16 - j)
    len1 = 4 * (j + 1)
    for t in range(64):
        btab[:, t] = -SHIFT if t < len0 - 4 else NEGBIG
    for t in range(32):
        btab[:, 64 + t] = -SHIFT if t < len1 - 4 else NEGBIG
    btab[:, 96] = -SHIFT
    btab[:, 97] = 1e-5
    btab[:, 98] = -0.5
    btab[:, 99] = 0.125
    return ident, maskp, btab


def kernel(L0_attn_x, L0_attn_w_q, L0_attn_w_k, L0_attn_w_v, L0_attn_w_o,
           L0_ffn_gate_w, L0_ffn_w1, L0_ffn_b1, L0_ffn_w2, L0_ffn_b2,
           L0_ffn_gamma, L0_ffn_beta):
    x = np.ascontiguousarray(np.asarray(L0_attn_x, np.float32))
    shared = {
        "x": x,
        "wq": np.ascontiguousarray(np.asarray(L0_attn_w_q, np.float32)),
        "wk": np.ascontiguousarray(np.asarray(L0_attn_w_k, np.float32)),
        "wv": np.ascontiguousarray(np.asarray(L0_attn_w_v, np.float32)),
        "wo": np.ascontiguousarray(np.asarray(L0_attn_w_o, np.float32)),
        "gw": np.ascontiguousarray(np.asarray(L0_ffn_gate_w, np.float32)),
        "w1": np.ascontiguousarray(np.asarray(L0_ffn_w1, np.float32)),
        "b1": np.ascontiguousarray(np.asarray(L0_ffn_b1, np.float32)),
        "gbc": np.ascontiguousarray(np.concatenate(
            [np.asarray(L0_ffn_gamma, np.float32).reshape(8, 128).T,
             np.asarray(L0_ffn_beta, np.float32).reshape(8, 128).T], axis=1)),
        "b1t": np.ascontiguousarray(
            np.asarray(L0_ffn_b1, np.float32).reshape(E * 32, 128).T),
        "w2": np.ascontiguousarray(np.asarray(L0_ffn_w2, np.float32)),
        "b2": np.ascontiguousarray(np.asarray(L0_ffn_b2, np.float32)),
    }
    in_maps = []
    for j in range(NCORE):
        ident, maskp, btab = _host_constants(j)
        rows = (15 - j, j)
        xo = np.stack([x[b, r * 512:(r + 1) * 512, :] for b in range(B) for r in rows])
        m = dict(shared)
        m["xown"] = np.ascontiguousarray(xo)
        m["ident"] = ident
        m["maskp"] = np.ascontiguousarray(maskp)
        m["btab"] = btab
        in_maps.append(m)

    nc = build_program()
    res = run_bass_kernel_spmd(nc, in_maps, core_ids=list(range(NCORE)))
    out = np.empty((B, T, D), np.float32)
    for j in range(NCORE):
        yj = np.asarray(res.results[j]["y"], np.float32)
        rows = (15 - j, j)
        for b in range(B):
            for si, r in enumerate(rows):
                out[b, r * 512:(r + 1) * 512, :] = yj[b * 2 + si]
    return out


# version 06
# speedup vs baseline: 1.1649x; 1.1649x over previous
import numpy as np
from contextlib import ExitStack

import concourse.bass as bass
import concourse.mybir as mybir
from concourse.bass_utils import run_bass_kernel_spmd

F32 = mybir.dt.float32
BF16 = mybir.dt.bfloat16
AF = mybir.ActivationFunctionType
ALU = mybir.AluOpType
AX = mybir.AxisListType

B, T, D = 2, 8192, 1024
NPAIR, E = 8, 8
NCORE = 8
PHASES = "CPOWAF"
DEBUG_TAPS = False
SHIFT = 32.0
NEGBIG = -30000.0
CAP = (64, 32)
GG = 4

ENGS = ("sync", "vector", "scalar", "tensor", "gpsimd")

WQ, WK, WV, LNF = 0, 8192, 16384, 24576
WO = 20608
QSL, KST, VST, AAB = 0, 4096, 8192, 12352
ONORM = 4096
XN, HHO, W1S, W2Q = 28800, 32896, 41088, 43136
XB, JUNK = 0, 8192
OACC = 0
SACC, WSTG = 8192, 12288
X1B, XOB, LNA, CW = 16384, 18432, 20480, 21504
LNT, KBUF, VBUF, QBUF = 0, 2048, 3072, 5152
WCST = 0
LN2N, JKF, CWB = 4096, 5120, 6144

COL_EPS, COL_NHALF, COL_QSC = 97, 98, 99


class Sched:
    def __init__(self):
        self.ops = {e: [] for e in ENGS}
        self.ctr = {s: 0 for s in ("qS", "qG", "qC", "tP", "tC", "tV", "tG")}
        self.last = {}

    def on(self, eng, fn):
        self.ops[eng].append(fn)

    def wait(self, eng, sem, val):
        if val is None or val <= 0:
            return
        self.on(eng, lambda E, s=sem, v=val: E.wait_ge(E._sems[s], v))

    def waitk(self, eng, key):
        tok = self.last.get(key)
        if tok is not None:
            self.wait(eng, tok[0], tok[1])

    def dma(self, eng, out, in_, sem):
        self.ctr[sem] += 16
        v = self.ctr[sem]
        self.on(eng, lambda E, o=out, i=in_, s=sem: E.dma_start(out=o, in_=i).then_inc(E._sems[s], 16))
        return v

    def op(self, eng, fn, sem=None):
        if sem is None:
            self.on(eng, fn)
            return None
        self.ctr[sem] += 1
        v = self.ctr[sem]
        def run(E, f=fn, s=sem):
            f(E).then_inc(E._sems[s], 1)
        self.on(eng, run)
        return v


def build_program():
    nc = bass.Bass("TRN2", target_bir_lowering=False, debug=False)
    ap = {}
    for name, shape in (("x", [B, T, D]), ("xown", [4, 512, D]),
                        ("wq", [16, D, 64]), ("wk", [16, D, 64]), ("wv", [16, D, 64]),
                        ("wo", [D, D]), ("gw", [D, E]),
                        ("w1", [E, D, 4096]), ("b1", [E, 4096]),
                        ("w2", [E, 4096, D]), ("b2", [E, D]),
                        ("gbc", [128, 16]), ("b1t", [128, 256]),
                        ("ident", [128, 128]), ("maskp", [4, 128, 512]),
                        ("btab", [128, 100])):
        ap[name] = nc.dram_tensor(name, shape, F32, kind="ExternalInput").ap()
    y = nc.dram_tensor("y", [4, 512, D], F32, kind="ExternalOutput").ap()
    taps = {}
    if DEBUG_TAPS:
        taps["t_kscr"] = nc.dram_tensor("t_kscr", [NPAIR, 128, T], BF16, kind="ExternalOutput").ap()
        taps["t_vscr"] = nc.dram_tensor("t_vscr", [T, 1040], BF16, kind="ExternalOutput").ap()
        taps["t_qscr"] = nc.dram_tensor("t_qscr", [4, 128, 4096], BF16, kind="ExternalOutput").ap()
        taps["t_x1"] = nc.dram_tensor("t_x1", [4, 512, D], F32, kind="ExternalOutput").ap()
        taps["t_w1bf"] = nc.dram_tensor("t_w1bf", [D, 4096], BF16, kind="ExternalOutput").ap()
        taps["t_w2bf"] = nc.dram_tensor("t_w2bf", [4096, D], BF16, kind="ExternalOutput").ap()

    scr = {
        "kscr": nc.dram_tensor("kscr", [B, NPAIR, 128, T], BF16, kind="Internal").ap(),
        "vscr": nc.dram_tensor("vscr", [B, T, 1040], BF16, kind="Internal").ap(),
        "qscr": nc.dram_tensor("qscr", [4, 128, 4096], BF16, kind="Internal").ap(),
        "kown": nc.dram_tensor("kown", [4, 128, 4096], BF16, kind="Internal").ap(),
        "vown": nc.dram_tensor("vown", [4, 512, 1040], BF16, kind="Internal").ap(),
        "x1scr": nc.dram_tensor("x1scr", [4, 512, D], F32, kind="Internal").ap(),
        "w1bf": nc.dram_tensor("w1bf", [E, D, 4096], BF16, kind="Internal").ap(),
        "w2bf": nc.dram_tensor("w2bf", [E, 4096, D], BF16, kind="Internal").ap(),
    }

    with ExitStack() as stk:
        En = stk.enter_context
        sb = {
            "AR1": En(nc.sbuf_tensor("AR1", [128, 51456], BF16)),
            "AR2": En(nc.sbuf_tensor("AR2", [128, 21760], F32)),
            "AR3": En(nc.sbuf_tensor("AR3", [128, 6272], BF16)),
            "c_ident": En(nc.sbuf_tensor("c_ident", [128, 128], BF16)),
            "c_maskp": En(nc.sbuf_tensor("c_maskp", [128, 2048], BF16)),
            "c_btab": En(nc.sbuf_tensor("c_btab", [128, 100], F32)),
            "c_gb": En(nc.sbuf_tensor("c_gb", [128, 16], F32)),
            "c_b1t": En(nc.sbuf_tensor("c_b1t", [128, 256], F32)),
            "c_b2": En(nc.sbuf_tensor("c_b2", [8, 1024], BF16)),
            "c_gw": En(nc.sbuf_tensor("c_gw", [128, 64], BF16)),
            "c_st": En(nc.sbuf_tensor("c_st", [128, 64], F32)),
            "c_onb": En(nc.sbuf_tensor("c_onb", [1, 64], BF16)),
            "c_cwt": En(nc.sbuf_tensor("c_cwt", [8, 512], BF16)),
            "c_f128": En(nc.sbuf_tensor("c_f128", [128, 128], F32)),
            "pS": En(nc.psum_tensor("pS", [128, 2048], F32)),
            "pC": En(nc.psum_tensor("pC", [128, 512], F32)),
            "pD": En(nc.psum_tensor("pD", [128, 512], F32)),
            "pW": En(nc.psum_tensor("pW", [128, 512], F32)),
            "pT": En(nc.psum_tensor("pT", [128, 1024], BF16)),
        }
        sems = {s: En(nc.semaphore(s)) for s in ("qS", "qG", "qC", "tP", "tC", "tV", "tG")}

        S = Sched()
        emit_all(S, ap, y, scr, sb, taps)

        block = En(nc.Block())
        def section(eng_name):
            def body(E):
                E._sems = sems
                for fn in S.ops[eng_name]:
                    fn(E)
            return body
        block.sync(section("sync"))
        block.vector(section("vector"))
        block.scalar(section("scalar"))
        block.tensor(section("tensor"))
        block.gpsimd(section("gpsimd"))
    return nc


def emit_all(S, ap, y, scr, sb, taps=None):
    AR1, AR2, AR3 = sb["AR1"], sb["AR2"], sb["AR3"]
    pS, pC, pD, pW, pT = sb["pS"], sb["pC"], sb["pD"], sb["pW"], sb["pT"]
    c_btab, c_st, c_ident, c_maskp = sb["c_btab"], sb["c_st"], sb["c_ident"], sb["c_maskp"]
    last = S.last

    cdp = [0]
    def cd_slot():
        pp = cdp[0] % 2
        cdp[0] += 1
        return (pC, ("pcd", 0)) if pp == 0 else (pD, ("pcd", 1))

    S.dma("sync", c_btab[:, :], ap["btab"], "qS")
    S.dma("sync", sb["c_f128"][:, :], ap["ident"], "qS")
    for mi in range(4):
        S.dma("sync", AR2[:, JUNK + mi * 512:JUNK + (mi + 1) * 512], ap["maskp"][mi], "qS")
    S.dma("sync", sb["c_gb"][:, :], ap["gbc"], "qS")
    S.dma("sync", sb["c_b1t"][:, :], ap["b1t"], "qS")
    S.dma("sync", AR2[0:8, JUNK + 2048:JUNK + 3072], ap["b2"], "qS")
    for c in range(8):
        S.dma("sync", AR2[:, JUNK + 3072 + c * 8:JUNK + 3072 + (c + 1) * 8],
              ap["gw"][c * 128:(c + 1) * 128, :], "qS")
    v_ld = S.ctr["qS"]

    S.wait("vector", "qS", v_ld)
    S.op("vector", lambda E: E.tensor_copy(c_ident[:, :], sb["c_f128"][:, :]))
    S.op("vector", lambda E: E.tensor_copy(c_maskp[:, :], AR2[:, JUNK:JUNK + 2048]))
    S.op("vector", lambda E: E.tensor_copy(sb["c_b2"][0:8, :], AR2[0:8, JUNK + 2048:JUNK + 3072]))
    S.op("vector", lambda E: E.tensor_copy(sb["c_gw"][:, :], AR2[:, JUNK + 3072:JUNK + 3136]))
    S.op("vector", lambda E: E.memset(sb["c_onb"][0:1, :], 1.0))
    for vb in range(2):
        off = VBUF + vb * 1040
        S.op("vector", lambda E, o=off: E.memset(
            AR3[:, o:o + 1040].rearrange("p (h a) -> p h a", a=65)[:, :, 64:65], 1.0))
    v_cast0 = S.op("vector", lambda E: E.engine_nop(), "tV")
    t_cV = v_cast0

    def stage_cast(loader, dst_off, ncols):
        loader()
        vq = S.ctr["qS"]
        S.wait("vector", "qS", vq)
        tv = S.op("vector", lambda E, o=dst_off, n=ncols: E.tensor_copy(
            AR1[:, o:o + n], AR2[:, XB:XB + n]), "tV")
        S.wait("sync", "tV", tv)
        return tv

    def pairs_loader(name):
        def go():
            for c in range(8):
                for pr in range(NPAIR):
                    blk = c * 8 + pr
                    src = ap[name][2 * pr:2 * pr + 2, c * 128:(c + 1) * 128, :] \
                        .rearrange("h p a -> p h a")
                    dst = AR2[:, XB + blk * 128:XB + (blk + 1) * 128] \
                        .rearrange("p (h a) -> p h a", a=64)
                    S.dma("sync", dst, src, "qS")
        return go
    stage_cast(pairs_loader("wq"), WQ, 8192)
    stage_cast(pairs_loader("wk"), WK, 8192)
    def wv_loader():
        for c in range(8):
            src = ap["wv"][:, c * 128:(c + 1) * 128, :].rearrange("h p a -> p h a")
            dst = AR2[:, XB + c * 1024:XB + (c + 1) * 1024] \
                .rearrange("p (h a) -> p h a", a=64)
            S.dma("sync", dst, src, "qS")
    stage_cast(wv_loader, WV, 8192)

    stat_slot = [0]

    def ln_block(src_ap, dst_ap, junk_ap):
        sl = stat_slot[0] % 8
        stat_slot[0] += 1
        cm, cs2, cv2, crs = sl * 8, sl * 8 + 1, sl * 8 + 2, sl * 8 + 3
        tv0 = S.op("vector", lambda E: E.scalar_tensor_tensor(
            junk_ap, src_ap, 1.0, src_ap, ALU.mult, ALU.mult,
            accum_out=c_st[:, cs2:cs2 + 1]), "tV")
        tvm = S.op("vector", lambda E: E.reduce_sum(
            c_st[:, cm:cm + 1], src_ap, axis=AX.X), "tV")
        S.wait("vector", "tV", tvm)
        S.op("vector", lambda E: E.tensor_scalar(
            c_st[:, cv2:cv2 + 1], c_st[:, cs2:cs2 + 1], 1.0 / 1024.0, None, ALU.mult))
        t2 = S.op("vector", lambda E: E.tensor_scalar(
            c_st[:, cm:cm + 1], c_st[:, cm:cm + 1], 1.0 / 1024.0, None, ALU.mult), "tV")
        S.wait("vector", "tV", t2)
        t3 = S.op("vector", lambda E: E.scalar_tensor_tensor(
            c_st[:, crs:crs + 1], c_st[:, cm:cm + 1], -1.0, c_st[:, cm:cm + 1],
            ALU.mult, ALU.mult), "tV")
        S.wait("vector", "tV", t3)
        t4 = S.op("vector", lambda E: E.tensor_add(
            c_st[:, cv2:cv2 + 1], c_st[:, cv2:cv2 + 1], c_st[:, crs:crs + 1]), "tV")
        S.wait("scalar", "tV", t4)
        t5 = S.op("scalar", lambda E: E.activation(
            c_st[:, crs:crs + 1], c_st[:, cv2:cv2 + 1], AF.Ln,
            bias=c_btab[:, COL_EPS:COL_EPS + 1]), "tC")
        S.wait("scalar", "tC", t5)
        tc2 = S.op("scalar", lambda E: E.activation(
            c_st[:, crs:crs + 1], c_st[:, crs:crs + 1], AF.Exp,
            scale=c_btab[:, COL_NHALF:COL_NHALF + 1]), "tC")
        S.wait("vector", "tC", tc2)
        tv5 = S.op("vector", lambda E: E.tensor_scalar(
            dst_ap, src_ap, c_st[:, cm:cm + 1], c_st[:, crs:crs + 1],
            ALU.subtract, ALU.mult), "tV")
        return tv5

    prst = {"grp": 0, "kq": 0}

    def proj_512(xsrc, kdst, vdst, qdst):
        gbuf = prst["grp"] % 2
        lnf_off = LNF + gbuf * 4096
        prst["grp"] += 1
        xoff = XB + gbuf * 4096
        S.waitk("sync", ("xb", gbuf))
        vx = S.dma("sync", AR2[:, xoff:xoff + 4096].rearrange("p (s d) -> p s d", d=1024),
                   xsrc.rearrange("(s p) d -> p s d", p=128), "qS")
        S.waitk("vector", ("lnfw", gbuf))
        for sub in range(4):
            lbuf = sub % 2
            loff = LNT + lbuf * 1024
            S.waitk("vector", ("lnt", lbuf))
            S.wait("vector", "qS", vx)
            S.wait("vector", "tV", t_cV)
            tln = ln_block(AR2[:, xoff + sub * 1024:xoff + (sub + 1) * 1024],
                           AR3[:, loff:loff + 1024], AR2[:, JUNK:JUNK + 1024])
            S.waitk("tensor", ("pT",))
            S.wait("tensor", "tV", tln)
            tp = None
            for c in range(8):
                tp = S.op("tensor", lambda E, c=c, o=loff: E.transpose(
                    pT[:, c * 128:(c + 1) * 128],
                    AR3[:, o + c * 128:o + (c + 1) * 128], c_ident[:, :]),
                    "tP" if c == 7 else None)
            last[("lnt", lbuf)] = ("tP", tp)
            S.wait("vector", "tP", tp)
            tv = S.op("vector", lambda E, s=sub, lo=lnf_off: E.tensor_copy(
                AR1[:, lo:lo + 4096].rearrange("p (c n) -> p c n", n=512)[:, :, s * 128:(s + 1) * 128],
                pT[:, :].rearrange("p (c n) -> p c n", n=128)), "tV")
            last[("pT",)] = ("tV", tv)
        v_lnf = S.ctr["tV"]
        last[("xb", gbuf)] = ("tV", S.ctr["tV"])

        def pair_proj(w_off, dsts, scale_col, tag):
            for pr in range(NPAIR):
                kb = prst["kq"] % 2
                prst["kq"] += 1
                ps, pkey = cd_slot()
                S.waitk("tensor", pkey)
                S.wait("tensor", "tV", v_lnf)
                tp = None
                for c in range(8):
                    tp = S.op("tensor", lambda E, c=c, pr=pr, ps=ps, wo=w_off, lo=lnf_off: E.matmul(
                        ps[:, :], AR1[:, wo + (c * 8 + pr) * 128:wo + (c * 8 + pr + 1) * 128],
                        AR1[:, lo + c * 512:lo + (c + 1) * 512],
                        start=(c == 0), stop=(c == 7)), "tP" if c == 7 else None)
                koff = (KBUF if tag == "k" else QBUF) + kb * 512
                if scale_col is None:
                    S.waitk("scalar", (tag + "buf", kb))
                    S.wait("scalar", "tP", tp)
                    tc = S.op("scalar", lambda E, ps=ps, ko=koff: E.activation(
                        AR3[:, ko:ko + 512], ps[:, :], AF.Copy), "tC")
                    last[pkey] = ("tC", tc)
                    S.wait("gpsimd", "tC", tc)
                else:
                    S.waitk("vector", (tag + "buf", kb))
                    S.wait("vector", "tP", tp)
                    tc = S.op("vector", lambda E, ps=ps, ko=koff: E.tensor_scalar(
                        AR3[:, ko:ko + 512], ps[:, :], 0.125, None, ALU.mult), "tV")
                    last[pkey] = ("tV", tc)
                    S.wait("gpsimd", "tV", tc)
                vg = S.dma("gpsimd", dsts[pr], AR3[:, koff:koff + 512], "qG")
                last[(tag + "buf", kb)] = ("qG", vg)

        pair_proj(WK, kdst, None, "k")
        if qdst is not None:
            pair_proj(WQ, qdst, COL_QSC, "q")

        for sub in range(4):
            S.wait("tensor", "tV", v_lnf)
            vb = sub % 2
            voff = VBUF + vb * 1040
            S.waitk("scalar", ("vbuf", vb))
            tcs = None
            for dh in range(2):
                S.waitk("tensor", ("pw",))
                tp = None
                for c in range(8):
                    tp = S.op("tensor", lambda E, c=c, dh=dh, s=sub, lo=lnf_off: E.matmul(
                        pW[:, :],
                        AR1[:, lo + c * 512 + s * 128:lo + c * 512 + (s + 1) * 128],
                        AR1[:, WV + c * 1024 + dh * 512:WV + c * 1024 + (dh + 1) * 512],
                        start=(c == 0), stop=(c == 7)), "tP" if c == 7 else None)
                S.wait("scalar", "tP", tp)
                tcs = S.op("scalar", lambda E, dh=dh, vo=voff: E.activation(
                    AR3[:, vo + dh * 520:vo + (dh + 1) * 520]
                    .rearrange("p (h a) -> p h a", a=65)[:, :, 0:64],
                    pW[:, :].rearrange("p (h a) -> p h a", a=64), AF.Copy), "tC")
                last[("pw",)] = ("tC", tcs)
            S.wait("gpsimd", "tC", tcs)
            vg = S.dma("gpsimd", vdst[sub], AR3[:, voff:voff + 1040], "qG")
            last[("vbuf", vb)] = ("qG", vg)
        last[("lnfw", gbuf)] = ("tP", S.ctr["tP"])

    for b in (range(B) if "P" in PHASES else ()):
        for g in range(16):
            proj_512(ap["x"][b, g * 512:(g + 1) * 512, :],
                     [scr["kscr"][b, pr, :, g * 512:(g + 1) * 512] for pr in range(NPAIR)],
                     [scr["vscr"][b, g * 512 + s * 128:g * 512 + (s + 1) * 128, :] for s in range(4)],
                     None)
    for gs in (range(4) if "O" in PHASES else ()):
        proj_512(ap["xown"][gs],
                 [scr["kown"][gs, :, pr * 512:(pr + 1) * 512] for pr in range(NPAIR)],
                 [scr["vown"][gs, s * 128:(s + 1) * 128, :] for s in range(4)],
                 [scr["qscr"][gs, :, pr * 512:(pr + 1) * 512] for pr in range(NPAIR)])
    t_PO_P = S.ctr["tP"]
    v_PO_G = S.ctr["qG"]

    S.wait("sync", "tP", t_PO_P)
    S.wait("vector", "tP", t_PO_P)
    def wo_loader():
        for pr in range(NPAIR):
            S.dma("sync", AR2[:, XB + pr * 1024:XB + (pr + 1) * 1024],
                  ap["wo"][pr * 128:(pr + 1) * 128, :], "qS")
    stage_cast(wo_loader, WO, 8192)
    t_wo_V = S.ctr["tV"]

    S.wait("gpsimd", "qG", v_PO_G)
    wblk = [0]

    def precast(src, dst):
        wb = wblk[0] % 2
        wblk[0] += 1
        S.waitk("gpsimd", ("wstg", wb))
        dst_st = AR2[:, WSTG + wb * 2048:WSTG + (wb + 1) * 2048]
        if len(src.shape) == 3:
            dst_st = dst_st.rearrange("p (s d) -> p s d", d=src.shape[2])
        vq = S.dma("gpsimd", dst_st, src, "qG")
        S.waitk("gpsimd", ("wcst", wb))
        S.wait("gpsimd", "qG", vq)
        tg = S.op("gpsimd", lambda E, wb=wb: E.tensor_copy(
            AR3[:, WCST + wb * 2048:WCST + (wb + 1) * 2048],
            AR2[:, WSTG + wb * 2048:WSTG + (wb + 1) * 2048]), "tG")
        last[("wstg", wb)] = ("tG", tg)
        src_st = AR3[:, WCST + wb * 2048:WCST + (wb + 1) * 2048]
        if len(dst.shape) == 3:
            src_st = src_st.rearrange("p (s d) -> p s d", d=dst.shape[2])
        vg = S.dma("gpsimd", dst, src_st, "qG")
        last[("wcst", wb)] = ("qG", vg)

    for e in (range(E) if "W" in PHASES else ()):
        for blk in range(8):
            for hf in range(2):
                precast(ap["w1"][e, blk * 128:(blk + 1) * 128, hf * 2048:(hf + 1) * 2048],
                        scr["w1bf"][e, blk * 128:(blk + 1) * 128, hf * 2048:(hf + 1) * 2048])
        for blk in range(16):
            precast(ap["w2"][e, blk * 256:(blk + 1) * 256, :].rearrange("(s p) d -> p s d", p=128),
                    scr["w2bf"][e, blk * 256:(blk + 1) * 256, :].rearrange("(s p) d -> p s d", p=128))
    v_W_G = S.ctr["qG"]
    t_W_G = S.ctr["tG"]

    ffst = {"w1": 0, "w2": 0, "hho": 0}

    def ffn_gen(g):
        for e in range(E):
            for hh in range(2):
                for oct_ in range(2):
                    hbase = hh * 16 + oct_ * 8
                    hslot = ffst["hho"] % 2
                    ffst["hho"] += 1
                    hoff = HHO + hslot * 4096
                    for hcl in range(8):
                        hc = hbase + hcl
                        wb = ffst["w1"] % 2
                        ffst["w1"] += 1
                        S.waitk("sync", ("w1s", wb))
                        vq = S.dma("sync", AR1[:, W1S + wb * 1024:W1S + (wb + 1) * 1024]
                                   .rearrange("p (c n) -> p c n", n=128),
                                   scr["w1bf"][e, :, hc * 128:(hc + 1) * 128]
                                   .rearrange("(c p) n -> p c n", p=128), "qC")
                        ps1, pkey = cd_slot()
                        S.waitk("tensor", pkey)
                        S.wait("tensor", "qC", vq)
                        tp = None
                        for c in range(8):
                            tp = S.op("tensor", lambda E, c=c, wb=wb, ps1=ps1: E.matmul(
                                ps1[:, :],
                                AR1[:, W1S + wb * 1024 + c * 128:W1S + wb * 1024 + (c + 1) * 128],
                                AR1[:, XN + c * 512:XN + (c + 1) * 512],
                                start=(c == 0), stop=(c == 7)), "tP" if c == 7 else None)
                        last[("w1s", wb)] = ("tP", tp)
                        S.waitk("scalar", ("hho", hslot, hcl))
                        S.wait("scalar", "tP", tp)
                        tc = S.op("scalar", lambda E, hcl=hcl, hc=hc, e=e, ps1=ps1, ho=hoff: E.activation(
                            AR1[:, ho + hcl * 512:ho + (hcl + 1) * 512], ps1[:, :], AF.Gelu,
                            bias=sb["c_b1t"][:, e * 32 + hc:e * 32 + hc + 1]), "tC")
                        last[pkey] = ("tC", tc)
                        yield
                    t_gel = S.ctr["tC"]
                    for dh in range(2):
                        wb2 = ffst["w2"] % 2
                        ffst["w2"] += 1
                        S.waitk("sync", ("w2q", wb2))
                        vq2 = S.dma("sync", AR1[:, W2Q + wb2 * 4096:W2Q + (wb2 + 1) * 4096]
                                    .rearrange("p (k n) -> p k n", n=512),
                                    scr["w2bf"][e, hbase * 128:(hbase + 8) * 128, dh * 512:(dh + 1) * 512]
                                    .rearrange("(k p) n -> p k n", p=128), "qS")
                        for sub in range(4):
                            S.waitk("tensor", ("pw",))
                            S.wait("tensor", "qS", vq2)
                            S.wait("tensor", "tC", t_gel)
                            tp = None
                            for kk in range(8):
                                tp = S.op("tensor", lambda E, kk=kk, sub=sub, wb2=wb2, ho=hoff: E.matmul(
                                    pW[:, :],
                                    AR1[:, ho + kk * 512 + sub * 128:ho + kk * 512 + (sub + 1) * 128],
                                    AR1[:, W2Q + wb2 * 4096 + kk * 512:W2Q + wb2 * 4096 + (kk + 1) * 512],
                                    start=(kk == 0), stop=(kk == 7)), "tP" if kk == 7 else None)
                            last[("w2q", wb2)] = ("tP", tp)
                            soff = SACC + sub * 1024 + dh * 512
                            cwo = CW + sub * 8 + e
                            S.wait("vector", "tP", tp)
                            tv = S.op("vector", lambda E, so=soff, cwo=cwo: E.scalar_tensor_tensor(
                                AR2[:, so:so + 512], pW[:, :], AR2[:, cwo:cwo + 1],
                                AR2[:, so:so + 512], ALU.mult, ALU.add), "tV")
                            last[("pw",)] = ("tV", tv)
                            yield
                    for hcl in range(8):
                        last[("hho", hslot, hcl)] = ("tP", S.ctr["tP"])
        tfin = S.ctr["tV"]
        S.wait("gpsimd", "tV", tfin)
        vg = S.dma("gpsimd", y[g].rearrange("(s p) d -> p s d", p=128),
                   AR2[:, SACC:SACC + 4096].rearrange("p (s d) -> p s d", d=1024), "qG")
        last[("sacc",)] = ("qG", vg)
        last[("xn",)] = ("tP", S.ctr["tP"])
        ffst["done", g] = True
        while True:
            yield

    def ffn_prep(g):
        S.waitk("sync", ("sacc",))
        S.waitk("sync", ("x1s",))
        vx = S.dma("sync", AR2[:, SACC:SACC + 4096].rearrange("p (s d) -> p s d", d=1024),
                   scr["x1scr"][g].rearrange("(s p) d -> p s d", p=128), "qS")
        for sub in range(4):
            soff = SACC + sub * 1024
            S.wait("vector", "qS", vx)
            tln1 = ln_block(AR2[:, soff:soff + 1024], AR2[:, LNA:LNA + 1024],
                            AR3[:, JKF:JKF + 1024])
            S.wait("vector", "tV", tln1)
            tva = S.op("vector", lambda E, so=soff: E.tensor_add(
                AR2[:, so:so + 1024], AR2[:, so:so + 1024], AR2[:, LNA:LNA + 1024]), "tV")
            S.wait("vector", "tV", tva)
            tln2 = ln_block(AR2[:, LNA:LNA + 1024], AR3[:, LN2N:LN2N + 1024],
                            AR3[:, JKF:JKF + 1024])
            S.waitk("tensor", ("pT",))
            S.wait("tensor", "tV", tln2)
            tp = None
            for c in range(8):
                tp = S.op("tensor", lambda E, c=c: E.transpose(
                    pT[:, c * 128:(c + 1) * 128],
                    AR3[:, LN2N + c * 128:LN2N + (c + 1) * 128], c_ident[:, :]),
                    "tP" if c == 7 else None)
            S.wait("vector", "tP", tp)
            tcx = None
            for c in range(8):
                tcx = S.op("vector", lambda E, c=c, sub=sub: E.tensor_scalar(
                    AR1[:, XN + c * 512 + sub * 128:XN + c * 512 + (sub + 1) * 128],
                    pT[:, c * 128:(c + 1) * 128],
                    sb["c_gb"][:, c:c + 1], sb["c_gb"][:, 8 + c:9 + c],
                    ALU.mult, ALU.add), "tV")
            last[("pT",)] = ("tV", tcx)
            S.waitk("tensor", ("pw",))
            S.wait("tensor", "tV", tcx)
            tpg = None
            for c in range(8):
                tpg = S.op("tensor", lambda E, c=c, sub=sub: E.matmul(
                    pW[:, 0:8],
                    AR1[:, XN + c * 512 + sub * 128:XN + c * 512 + (sub + 1) * 128],
                    sb["c_gw"][:, c * 8:(c + 1) * 8],
                    start=(c == 0), stop=(c == 7)), "tP" if c == 7 else None)
            cwo = CW + sub * 8
            sl = stat_slot[0] % 8
            stat_slot[0] += 1
            cm1, cm1n, cm2, cs2 = sl * 8, sl * 8 + 1, sl * 8 + 2, sl * 8 + 3
            S.wait("vector", "tP", tpg)
            w1_ = S.op("vector", lambda E, cm1=cm1: E.reduce_max(
                c_st[:, cm1:cm1 + 1], pW[:, 0:8], axis=AX.X), "tV")
            S.wait("vector", "tV", w1_)
            w2_ = S.op("vector", lambda E, cm1n=cm1n, cm1=cm1: E.tensor_scalar(
                c_st[:, cm1n:cm1n + 1], c_st[:, cm1:cm1 + 1], -1.0, None, ALU.mult), "tV")
            w3_ = S.op("vector", lambda E: E.tensor_copy(
                AR2[:, XOB:XOB + 8], pW[:, 0:8]), "tV")
            S.wait("vector", "tV", w3_)
            w4_ = S.op("vector", lambda E, cm1=cm1: E.tensor_scalar(
                AR2[:, XOB + 8:XOB + 16], AR2[:, XOB:XOB + 8],
                c_st[:, cm1:cm1 + 1], None, ALU.is_equal), "tV")
            S.wait("vector", "tV", w4_)
            w5_ = S.op("vector", lambda E: E.scalar_tensor_tensor(
                AR2[:, XOB + 8:XOB + 16], AR2[:, XOB + 8:XOB + 16], -1e30,
                AR2[:, XOB:XOB + 8], ALU.mult, ALU.add), "tV")
            S.wait("vector", "tV", w5_)
            w6_ = S.op("vector", lambda E, cm2=cm2: E.reduce_max(
                c_st[:, cm2:cm2 + 1], AR2[:, XOB + 8:XOB + 16], axis=AX.X), "tV")
            S.wait("vector", "tV", w6_)
            tvm = S.op("vector", lambda E, cm2=cm2: E.tensor_scalar(
                AR2[:, XOB + 8:XOB + 16], AR2[:, XOB:XOB + 8],
                c_st[:, cm2:cm2 + 1], None, ALU.is_ge), "tV")
            S.wait("scalar", "tV", max(tvm, w2_))
            tce = S.op("scalar", lambda E, cm1n=cm1n: E.activation(
                AR2[:, XOB + 16:XOB + 24], AR2[:, XOB:XOB + 8], AF.Exp,
                bias=c_st[:, cm1n:cm1n + 1]), "tC")
            S.wait("vector", "tC", tce)
            w7_ = S.op("vector", lambda E: E.tensor_mul(
                AR2[:, XOB + 16:XOB + 24], AR2[:, XOB + 16:XOB + 24],
                AR2[:, XOB + 8:XOB + 16]), "tV")
            S.wait("vector", "tV", w7_)
            w8_ = S.op("vector", lambda E, cs2=cs2: E.reduce_sum(
                c_st[:, cs2:cs2 + 1], AR2[:, XOB + 16:XOB + 24], axis=AX.X), "tV")
            S.wait("vector", "tV", w8_)
            w9_ = S.op("vector", lambda E, cs2=cs2: E.reciprocal(
                c_st[:, cs2 + 1:cs2 + 2], c_st[:, cs2:cs2 + 1]), "tV")
            S.wait("vector", "tV", w9_)
            tvc = S.op("vector", lambda E, cwo=cwo, cs2=cs2: E.tensor_scalar(
                AR2[:, cwo:cwo + 8], AR2[:, XOB + 16:XOB + 24],
                c_st[:, cs2 + 1:cs2 + 2], None, ALU.mult), "tV")
            S.wait("vector", "tV", tvc)
            tvb = S.op("vector", lambda E, cwo=cwo: E.tensor_copy(
                AR3[:, CWB:CWB + 8], AR2[:, cwo:cwo + 8]), "tV")
            S.waitk("tensor", ("pcwt",))
            S.wait("tensor", "tV", tvb)
            tpt = S.op("tensor", lambda E: E.transpose(
                pT[0:8, 0:128], AR3[:, CWB:CWB + 8], c_ident[:, :]), "tP")
            S.wait("vector", "tP", tpt)
            tvt = S.op("vector", lambda E, sub=sub: E.tensor_copy(
                sb["c_cwt"][0:8, sub * 128:(sub + 1) * 128], pT[0:8, 0:128]), "tV")
            last[("pcwt",)] = ("tV", tvt)
            S.wait("tensor", "tV", tvt)
            for dh in range(2):
                S.waitk("tensor", ("pw",))
                tpb = S.op("tensor", lambda E, dh=dh, sub=sub: E.matmul(
                    pW[:, :],
                    sb["c_cwt"][0:8, sub * 128:(sub + 1) * 128],
                    sb["c_b2"][0:8, dh * 512:(dh + 1) * 512],
                    start=True, stop=True), "tP")
                S.wait("vector", "tP", tpb)
                tvb2 = S.op("vector", lambda E, dh=dh, so=soff: E.tensor_add(
                    AR2[:, so + dh * 512:so + (dh + 1) * 512],
                    AR2[:, so + dh * 512:so + (dh + 1) * 512], pW[:, :]), "tV")
                last[("pw",)] = ("tV", tvb2)
        last[("xob", 0)] = ("tV", S.ctr["tV"])
        last[("xob", 1)] = ("tV", S.ctr["tV"])

    for k in (("qsl",), ("kvst",), ("xn",)):
        last[k] = ("tP", t_PO_P)
    aabc = [0]
    ofirst = [True] * 16
    pend = {"gen": None}

    def pump(n):
        if pend["gen"] is not None:
            for _ in range(n):
                next(pend["gen"])

    def drain(g):
        while pend["gen"] is not None and not ffst.get(("done", g)):
            next(pend["gen"])

    def attn_group(s, grp, diag):
        for pr in range(NPAIR):
            pump(2)
            ab = aabc[0] % 2
            aabc[0] += 1
            aoff = AAB + ab * 4096
            S.waitk("scalar", ("aab", ab))
            texp = []
            for ih in range(2):
                S.waitk("tensor", ("sps",))
                for i in (2 * ih, 2 * ih + 1):
                    po = (i % 2) * 1024
                    for h in range(2):
                        inc = (h == 1) and not diag
                        S.op("tensor", lambda E, h=h, i=i, pr=pr, po=po: E.matmul(
                            pS[:, po + h * 512:po + (h + 1) * 512],
                            AR1[h * 64:(h + 1) * 64, KST + pr * 512 + i * 128:KST + pr * 512 + (i + 1) * 128],
                            AR1[h * 64:(h + 1) * 64, QSL + pr * 512:QSL + (pr + 1) * 512],
                            start=True, stop=(not diag), tile_position=(h * 64, 0)),
                            "tP" if inc else None)
                        if diag:
                            S.op("tensor", lambda E, h=h, i=i, po=po: E.matmul(
                                pS[:, po + h * 512:po + (h + 1) * 512], c_ident[:, :],
                                c_maskp[:, i * 512:(i + 1) * 512],
                                start=False, stop=True), "tP" if h == 1 else None)
                tp = S.ctr["tP"]
                S.wait("scalar", "tP", tp)
                bcol = 96 if diag else (grp if s == 0 else 24 + grp)
                tc = S.op("scalar", lambda E, ih=ih, ao=aoff, bc=bcol: E.activation(
                    AR1[:, ao + ih * 2048:ao + (ih + 1) * 2048], pS[:, :], AF.Exp,
                    bias=c_btab[:, bc:bc + 1]), "tC")
                texp.append(tc)
                last[("sps",)] = ("tC", tc)
            for h in range(2):
                psz, pkey = cd_slot()
                S.waitk("tensor", pkey)
                S.wait("tensor", "tC", texp[-1])
                tp_av = None
                for i in range(GG):
                    tp_av = S.op("tensor", lambda E, i=i, h=h, pr=pr, psz=psz, ao=aoff: E.matmul(
                        psz[0:65, :],
                        AR1[:, VST + i * 1040 + (2 * pr + h) * 65:VST + i * 1040 + (2 * pr + h) * 65 + 65],
                        AR1[:, ao + i * 1024 + h * 512:ao + i * 1024 + (h + 1) * 512],
                        start=(i == 0), stop=(i == GG - 1)), "tP" if i == GG - 1 else None)
                slot = pr * 2 + h
                ooff = OACC + slot * 512
                S.wait("vector", "tP", tp_av)
                if ofirst[slot]:
                    tv = S.op("vector", lambda E, o=ooff, psz=psz: E.tensor_copy(
                        AR2[0:65, o:o + 512], psz[0:65, :]), "tV")
                    ofirst[slot] = False
                else:
                    tv = S.op("vector", lambda E, o=ooff, psz=psz: E.tensor_add(
                        AR2[0:65, o:o + 512], AR2[0:65, o:o + 512], psz[0:65, :]), "tV")
                last[pkey] = ("tV", tv)
            last[("aab", ab)] = ("tP", S.ctr["tP"])

    for bs in (range(4) if "A" in PHASES else ()):
        b, s = bs // 2, bs % 2
        S.waitk("sync", ("qsl",))
        S.wait("sync", "qG", v_PO_G)
        vq = S.dma("sync", AR1[:, QSL:QSL + 4096], scr["qscr"][bs], "qS")
        S.wait("tensor", "qS", vq)
        S.wait("tensor", "tV", t_wo_V)
        for i in range(16):
            ofirst[i] = True
        ngrp = CAP[s] // GG
        for grp in range(ngrp + 1):
            diag = grp == ngrp
            S.waitk("sync", ("kvst",))
            if diag:
                S.dma("sync", AR1[:, KST:KST + 4096], scr["kown"][bs], "qS")
                vv = S.dma("sync", AR1[:, VST:VST + 4160].rearrange("p (i d) -> p i d", d=1040),
                           scr["vown"][bs].rearrange("(i p) d -> p i d", p=128), "qS")
            else:
                S.dma("sync", AR1[:, KST:KST + 4096].rearrange("p (r t) -> p r t", t=512),
                      scr["kscr"][b, :, :, grp * 512:(grp + 1) * 512]
                      .rearrange("r p t -> p r t"), "qS")
                vv = S.dma("sync", AR1[:, VST:VST + 4160].rearrange("p (i d) -> p i d", d=1040),
                           scr["vscr"][b, grp * 512:(grp + 1) * 512, :]
                           .rearrange("(i p) d -> p i d", p=128), "qS")
            S.wait("tensor", "qS", vv)
            attn_group(s, grp, diag)
            last[("kvst",)] = ("tP", S.ctr["tP"])
            pump(2 if s == 0 else 14)

        for pr in range(NPAIR):
            for h in range(2):
                slot = pr * 2 + h
                ooff = OACC + slot * 512
                S.waitk("vector", ("zrw",))
                z1 = S.op("vector", lambda E, o=ooff: E.reciprocal(
                    AR2[0:1, LNA:LNA + 512], AR2[64:65, o:o + 512]), "tV")
                S.wait("vector", "tV", z1)
                z2 = S.op("vector", lambda E: E.tensor_copy(
                    AR3[0:1, JKF:JKF + 512], AR2[0:1, LNA:LNA + 512]), "tV")
                S.wait("vector", "tV", z2)
                z3 = S.op("vector", lambda E: E.tensor_sub(
                    AR2[0:1, LNA:LNA + 512], AR2[0:1, LNA:LNA + 512], AR3[0:1, JKF:JKF + 512]), "tV")
                S.wait("vector", "tV", z3)
                tvr = S.op("vector", lambda E: E.tensor_copy(
                    AR3[0:1, JKF + 512:JKF + 1024], AR2[0:1, LNA:LNA + 512]), "tV")
                psz, pkey = cd_slot()
                S.waitk("tensor", pkey)
                S.wait("tensor", "tV", tvr)
                S.op("tensor", lambda E, psz=psz: E.matmul(
                    psz[0:64, :], sb["c_onb"][0:1, :], AR3[0:1, JKF:JKF + 512],
                    start=True, stop=False), None)
                tpb = S.op("tensor", lambda E, psz=psz: E.matmul(
                    psz[0:64, :], sb["c_onb"][0:1, :], AR3[0:1, JKF + 512:JKF + 1024],
                    start=False, stop=True), "tP")
                last[("zrw",)] = ("tP", tpb)
                S.wait("vector", "tP", tpb)
                tvn = S.op("vector", lambda E, o=ooff, h=h, pr=pr, psz=psz: E.tensor_mul(
                    AR1[h * 64:(h + 1) * 64, ONORM + pr * 512:ONORM + (pr + 1) * 512],
                    AR2[0:64, o:o + 512], psz[0:64, :]), "tV")
                last[pkey] = ("tV", tvn)
            pump(6)
        v_on = S.ctr["tV"]

        for sub in range(4):
            xb2 = sub % 2
            S.waitk("sync", ("xob", xb2))
            vx = S.dma("sync", AR2[:, XOB + xb2 * 1024:XOB + (xb2 + 1) * 1024],
                       ap["xown"][bs, sub * 128:(sub + 1) * 128, :], "qS")
            tvs = None
            for dh in range(2):
                S.waitk("tensor", ("pw",))
                S.wait("tensor", "tV", v_on)
                tp = None
                for pr in range(NPAIR):
                    tp = S.op("tensor", lambda E, dh=dh, pr=pr, sub=sub: E.matmul(
                        pW[:, :],
                        AR1[:, ONORM + pr * 512 + sub * 128:ONORM + pr * 512 + (sub + 1) * 128],
                        AR1[:, WO + pr * 1024 + dh * 512:WO + pr * 1024 + (dh + 1) * 512],
                        start=(pr == 0), stop=(pr == 7)), "tP" if pr == 7 else None)
                x1b = sub % 2
                S.waitk("vector", ("x1b", x1b))
                S.wait("vector", "tP", tp)
                S.wait("vector", "qS", vx)
                tvs = S.op("vector", lambda E, dh=dh, x1b=x1b, xb2=xb2: E.tensor_add(
                    AR2[:, X1B + x1b * 1024 + dh * 512:X1B + x1b * 1024 + (dh + 1) * 512],
                    pW[:, :],
                    AR2[:, XOB + xb2 * 1024 + dh * 512:XOB + xb2 * 1024 + (dh + 1) * 512]), "tV")
                last[("pw",)] = ("tV", tvs)
            last[("xob", xb2)] = ("tV", tvs)
            S.wait("scalar", "tV", tvs)
            vg = S.dma("scalar", scr["x1scr"][bs, sub * 128:(sub + 1) * 128, :],
                       AR2[:, X1B + (sub % 2) * 1024:X1B + (sub % 2 + 1) * 1024], "qC")
            last[("x1b", sub % 2)] = ("qC", vg)
            pump(6)
        last[("qsl",)] = ("tP", S.ctr["tP"])
        last[("kvst",)] = ("tP", S.ctr["tP"])
        last[("x1s",)] = ("qC", S.ctr["qC"])

        if "F" in PHASES:
            if pend["gen"] is not None:
                drain(bs - 1)
            x1done = S.ctr["qG"]
            S.wait("sync", "qG", x1done)
            S.wait("sync", "tG", t_W_G)
            S.wait("scalar", "qG", v_W_G)
            ffn_prep(bs)
            pend["gen"] = ffn_gen(bs)
    if "F" in PHASES and pend["gen"] is not None:
        drain(3)

    if taps:
        S.wait("gpsimd", "qG", S.ctr["qG"])
        S.wait("gpsimd", "qC", S.ctr["qC"])
        S.wait("gpsimd", "tV", S.ctr["tV"])
        S.dma("gpsimd", taps["t_kscr"], scr["kscr"][0], "qG")
        S.dma("gpsimd", taps["t_vscr"], scr["vscr"][0], "qG")
        S.dma("gpsimd", taps["t_qscr"], scr["qscr"][:, :, :], "qG")
        S.dma("gpsimd", taps["t_x1"], scr["x1scr"][:, :, :], "qG")
        S.dma("gpsimd", taps["t_w1bf"], scr["w1bf"][0], "qG")
        S.dma("gpsimd", taps["t_w2bf"], scr["w2bf"][0], "qG")
    S.wait("gpsimd", "qG", S.ctr["qG"])


def _host_constants(j):
    ident = np.eye(128, dtype=np.float32)
    maskp = np.zeros((4, 128, 512), np.float32)
    for i in range(4):
        kt = i * 128 + np.arange(128)[:, None]
        q = np.arange(512)[None, :]
        maskp[i] = np.where(kt <= q, 0.0, NEGBIG)
    btab = np.zeros((128, 100), np.float32)
    len0 = 4 * (16 - j)
    len1 = 4 * (j + 1)
    for gg in range(16):
        btab[:, gg] = -SHIFT if gg * 4 < len0 - 4 else NEGBIG
    for gg in range(8):
        btab[:, 24 + gg] = -SHIFT if gg * 4 < len1 - 4 else NEGBIG
    btab[:, 96] = -SHIFT
    btab[:, 97] = 1e-5
    btab[:, 98] = -0.5
    btab[:, 99] = 0.125
    return ident, maskp, btab


def kernel(L0_attn_x, L0_attn_w_q, L0_attn_w_k, L0_attn_w_v, L0_attn_w_o,
           L0_ffn_gate_w, L0_ffn_w1, L0_ffn_b1, L0_ffn_w2, L0_ffn_b2,
           L0_ffn_gamma, L0_ffn_beta):
    x = np.ascontiguousarray(np.asarray(L0_attn_x, np.float32))
    shared = {
        "x": x,
        "wq": np.ascontiguousarray(np.asarray(L0_attn_w_q, np.float32)),
        "wk": np.ascontiguousarray(np.asarray(L0_attn_w_k, np.float32)),
        "wv": np.ascontiguousarray(np.asarray(L0_attn_w_v, np.float32)),
        "wo": np.ascontiguousarray(np.asarray(L0_attn_w_o, np.float32)),
        "gw": np.ascontiguousarray(np.asarray(L0_ffn_gate_w, np.float32)),
        "w1": np.ascontiguousarray(np.asarray(L0_ffn_w1, np.float32)),
        "b1": np.ascontiguousarray(np.asarray(L0_ffn_b1, np.float32)),
        "gbc": np.ascontiguousarray(np.concatenate(
            [np.asarray(L0_ffn_gamma, np.float32).reshape(8, 128).T,
             np.asarray(L0_ffn_beta, np.float32).reshape(8, 128).T], axis=1)),
        "b1t": np.ascontiguousarray(
            np.asarray(L0_ffn_b1, np.float32).reshape(E * 32, 128).T),
        "w2": np.ascontiguousarray(np.asarray(L0_ffn_w2, np.float32)),
        "b2": np.ascontiguousarray(np.asarray(L0_ffn_b2, np.float32)),
    }
    in_maps = []
    for j in range(NCORE):
        ident, maskp, btab = _host_constants(j)
        rows = (15 - j, j)
        xo = np.stack([x[b, r * 512:(r + 1) * 512, :] for b in range(B) for r in rows])
        m = dict(shared)
        m["xown"] = np.ascontiguousarray(xo)
        m["ident"] = ident
        m["maskp"] = np.ascontiguousarray(maskp)
        m["btab"] = btab
        in_maps.append(m)

    nc = build_program()
    res = run_bass_kernel_spmd(nc, in_maps, core_ids=list(range(NCORE)))
    out = np.empty((B, T, D), np.float32)
    for j in range(NCORE):
        yj = np.asarray(res.results[j]["y"], np.float32)
        rows = (15 - j, j)
        for b in range(B):
            for si, r in enumerate(rows):
                out[b, r * 512:(r + 1) * 512, :] = yj[b * 2 + si]
    return out


# version 08
# speedup vs baseline: 1.3461x; 1.1556x over previous
import numpy as np
from contextlib import ExitStack

import concourse.bass as bass
import concourse.mybir as mybir
from concourse.bass_utils import run_bass_kernel_spmd

F32 = mybir.dt.float32
BF16 = mybir.dt.bfloat16
AF = mybir.ActivationFunctionType
ALU = mybir.AluOpType
AX = mybir.AxisListType

B, T, D = 2, 8192, 1024
NPAIR, E = 8, 8
NCORE = 8
PHASES = "CPOWAF"
DEBUG_TAPS = False
SHIFT = 32.0
NEGBIG = -30000.0
CAP = (64, 32)
GG = 4

ENGS = ("sync", "vector", "scalar", "tensor", "gpsimd")

WQ, WK, WV, LNF = 0, 8192, 16384, 24576
WO = 20608
QSL, KST, VST, AAB = 0, 4096, 8192, 12352
ONORM = 4096
XN, HHO, W1S, W2Q = 28800, 32896, 41088, 43136
XB, JUNK = 0, 8192
OACC = 0
SACC, WSTG = 8192, 12288
X1B, XOB, LNA, CW = 16384, 18432, 20480, 21504
LNT, KBUF, VBUF, QBUF = 0, 2048, 3072, 5152
WCST = 0
LN2N, JKF, CWB = 4096, 5120, 6144

COL_EPS, COL_NHALF, COL_QSC = 97, 98, 99


class Sched:
    def __init__(self):
        self.ops = {e: [] for e in ENGS}
        self.ctr = {s: 0 for s in ("qS", "qG", "qC", "tP", "tC", "tV", "tG")}
        self.last = {}

    def on(self, eng, fn):
        self.ops[eng].append(fn)

    def wait(self, eng, sem, val):
        if val is None or val <= 0:
            return
        self.on(eng, lambda E, s=sem, v=val: E.wait_ge(E._sems[s], v))

    def waitk(self, eng, key):
        tok = self.last.get(key)
        if tok is not None:
            self.wait(eng, tok[0], tok[1])

    def dma(self, eng, out, in_, sem):
        self.ctr[sem] += 16
        v = self.ctr[sem]
        self.on(eng, lambda E, o=out, i=in_, s=sem: E.dma_start(out=o, in_=i).then_inc(E._sems[s], 16))
        return v

    def op(self, eng, fn, sem=None):
        if sem is None:
            self.on(eng, fn)
            return None
        self.ctr[sem] += 1
        v = self.ctr[sem]
        def run(E, f=fn, s=sem):
            f(E).then_inc(E._sems[s], 1)
        self.on(eng, run)
        return v


def build_program():
    nc = bass.Bass("TRN2", target_bir_lowering=False, debug=False)
    ap = {}
    for name, shape in (("x", [B, T, D]), ("xown", [4, 512, D]),
                        ("wq", [16, D, 64]), ("wk", [16, D, 64]), ("wv", [16, D, 64]),
                        ("wo", [D, D]), ("gw", [D, E]),
                        ("w1", [E, D, 4096]), ("b1", [E, 4096]),
                        ("w2", [E, 4096, D]), ("b2", [E, D]),
                        ("gbc", [128, 16]), ("b1t", [128, 256]),
                        ("ident", [128, 128]), ("maskp", [4, 128, 512]),
                        ("btab", [128, 100])):
        ap[name] = nc.dram_tensor(name, shape, F32, kind="ExternalInput").ap()
    y = nc.dram_tensor("y", [4, 512, D], F32, kind="ExternalOutput").ap()
    taps = {}
    if DEBUG_TAPS:
        taps["t_kscr"] = nc.dram_tensor("t_kscr", [NPAIR, 128, T], BF16, kind="ExternalOutput").ap()
        taps["t_vscr"] = nc.dram_tensor("t_vscr", [T, 1040], BF16, kind="ExternalOutput").ap()
        taps["t_qscr"] = nc.dram_tensor("t_qscr", [4, 128, 4096], BF16, kind="ExternalOutput").ap()
        taps["t_x1"] = nc.dram_tensor("t_x1", [4, 512, D], F32, kind="ExternalOutput").ap()
        taps["t_w1bf"] = nc.dram_tensor("t_w1bf", [D, 4096], BF16, kind="ExternalOutput").ap()
        taps["t_w2bf"] = nc.dram_tensor("t_w2bf", [4096, D], BF16, kind="ExternalOutput").ap()

    scr = {
        "kscr": nc.dram_tensor("kscr", [B, NPAIR, 128, T], BF16, kind="Internal").ap(),
        "vscr": nc.dram_tensor("vscr", [B, T, 1040], BF16, kind="Internal").ap(),
        "qscr": nc.dram_tensor("qscr", [4, 128, 4096], BF16, kind="Internal").ap(),
        "kown": nc.dram_tensor("kown", [4, 128, 4096], BF16, kind="Internal").ap(),
        "vown": nc.dram_tensor("vown", [4, 512, 1040], BF16, kind="Internal").ap(),
        "x1scr": nc.dram_tensor("x1scr", [4, 512, D], F32, kind="Internal").ap(),
        "w1bf": nc.dram_tensor("w1bf", [E, D, 4096], BF16, kind="Internal").ap(),
        "w2bf": nc.dram_tensor("w2bf", [E, 4096, D], BF16, kind="Internal").ap(),
    }

    with ExitStack() as stk:
        En = stk.enter_context
        sb = {
            "AR1": En(nc.sbuf_tensor("AR1", [128, 51456], BF16)),
            "AR2": En(nc.sbuf_tensor("AR2", [128, 21760], F32)),
            "AR3": En(nc.sbuf_tensor("AR3", [128, 6272], BF16)),
            "c_ident": En(nc.sbuf_tensor("c_ident", [128, 128], BF16)),
            "c_maskp": En(nc.sbuf_tensor("c_maskp", [128, 2048], BF16)),
            "c_btab": En(nc.sbuf_tensor("c_btab", [128, 100], F32)),
            "c_gb": En(nc.sbuf_tensor("c_gb", [128, 16], F32)),
            "c_b1t": En(nc.sbuf_tensor("c_b1t", [128, 256], F32)),
            "c_b2": En(nc.sbuf_tensor("c_b2", [8, 1024], BF16)),
            "c_gw": En(nc.sbuf_tensor("c_gw", [128, 64], BF16)),
            "c_st": En(nc.sbuf_tensor("c_st", [128, 64], F32)),
            "c_onb": En(nc.sbuf_tensor("c_onb", [1, 64], BF16)),
            "c_cwt": En(nc.sbuf_tensor("c_cwt", [8, 512], BF16)),
            "c_f128": En(nc.sbuf_tensor("c_f128", [128, 128], F32)),
            "pS": En(nc.psum_tensor("pS", [128, 2048], F32)),
            "pC": En(nc.psum_tensor("pC", [128, 512], F32)),
            "pD": En(nc.psum_tensor("pD", [128, 512], F32)),
            "pW": En(nc.psum_tensor("pW", [128, 512], F32)),
            "pT": En(nc.psum_tensor("pT", [128, 1024], BF16)),
        }
        sems = {s: En(nc.semaphore(s)) for s in ("qS", "qG", "qC", "tP", "tC", "tV", "tG")}

        S = Sched()
        emit_all(S, ap, y, scr, sb, taps)

        block = En(nc.Block())
        def section(eng_name):
            def body(E):
                E._sems = sems
                for fn in S.ops[eng_name]:
                    fn(E)
            return body
        block.sync(section("sync"))
        block.vector(section("vector"))
        block.scalar(section("scalar"))
        block.tensor(section("tensor"))
        block.gpsimd(section("gpsimd"))
    return nc


def emit_all(S, ap, y, scr, sb, taps=None):
    AR1, AR2, AR3 = sb["AR1"], sb["AR2"], sb["AR3"]
    pS, pC, pD, pW, pT = sb["pS"], sb["pC"], sb["pD"], sb["pW"], sb["pT"]
    c_btab, c_st, c_ident, c_maskp = sb["c_btab"], sb["c_st"], sb["c_ident"], sb["c_maskp"]
    last = S.last

    cdp = [0]
    def cd_slot():
        pp = cdp[0] % 2
        cdp[0] += 1
        return (pC, ("pcd", 0)) if pp == 0 else (pD, ("pcd", 1))

    S.dma("sync", c_btab[:, :], ap["btab"], "qS")
    S.dma("sync", sb["c_f128"][:, :], ap["ident"], "qS")
    for mi in range(4):
        S.dma("sync", AR2[:, JUNK + mi * 512:JUNK + (mi + 1) * 512], ap["maskp"][mi], "qS")
    S.dma("sync", sb["c_gb"][:, :], ap["gbc"], "qS")
    S.dma("sync", sb["c_b1t"][:, :], ap["b1t"], "qS")
    S.dma("sync", AR2[0:8, JUNK + 2048:JUNK + 3072], ap["b2"], "qS")
    for c in range(8):
        S.dma("sync", AR2[:, JUNK + 3072 + c * 8:JUNK + 3072 + (c + 1) * 8],
              ap["gw"][c * 128:(c + 1) * 128, :], "qS")
    v_ld = S.ctr["qS"]

    S.wait("vector", "qS", v_ld)
    S.op("vector", lambda E: E.tensor_copy(c_ident[:, :], sb["c_f128"][:, :]))
    S.op("vector", lambda E: E.tensor_copy(c_maskp[:, :], AR2[:, JUNK:JUNK + 2048]))
    S.op("vector", lambda E: E.tensor_copy(sb["c_b2"][0:8, :], AR2[0:8, JUNK + 2048:JUNK + 3072]))
    S.op("vector", lambda E: E.tensor_copy(sb["c_gw"][:, :], AR2[:, JUNK + 3072:JUNK + 3136]))
    S.op("vector", lambda E: E.memset(sb["c_onb"][0:1, :], 1.0))
    for vb in range(2):
        off = VBUF + vb * 1040
        S.op("vector", lambda E, o=off: E.memset(
            AR3[:, o:o + 1040].rearrange("p (h a) -> p h a", a=65)[:, :, 64:65], 1.0))
    v_cast0 = S.op("vector", lambda E: E.engine_nop(), "tV")
    t_cV = v_cast0

    def stage_cast(loader, dst_off, ncols):
        loader()
        vq = S.ctr["qS"]
        S.wait("vector", "qS", vq)
        tv = S.op("vector", lambda E, o=dst_off, n=ncols: E.tensor_copy(
            AR1[:, o:o + n], AR2[:, XB:XB + n]), "tV")
        S.wait("sync", "tV", tv)
        return tv

    def pairs_loader(name):
        def go():
            for c in range(8):
                for pr in range(NPAIR):
                    blk = c * 8 + pr
                    src = ap[name][2 * pr:2 * pr + 2, c * 128:(c + 1) * 128, :] \
                        .rearrange("h p a -> p h a")
                    dst = AR2[:, XB + blk * 128:XB + (blk + 1) * 128] \
                        .rearrange("p (h a) -> p h a", a=64)
                    S.dma("sync", dst, src, "qS")
        return go
    stage_cast(pairs_loader("wq"), WQ, 8192)
    stage_cast(pairs_loader("wk"), WK, 8192)
    def wv_loader():
        for c in range(8):
            src = ap["wv"][:, c * 128:(c + 1) * 128, :].rearrange("h p a -> p h a")
            dst = AR2[:, XB + c * 1024:XB + (c + 1) * 1024] \
                .rearrange("p (h a) -> p h a", a=64)
            S.dma("sync", dst, src, "qS")
    stage_cast(wv_loader, WV, 8192)

    stat_slot = [0]

    def ln_block(src_ap, dst_ap, junk_ap):
        sl = stat_slot[0] % 8
        stat_slot[0] += 1
        cm, cs2, cv2, crs = sl * 8, sl * 8 + 1, sl * 8 + 2, sl * 8 + 3
        tv0 = S.op("vector", lambda E: E.scalar_tensor_tensor(
            junk_ap, src_ap, 1.0, src_ap, ALU.mult, ALU.mult,
            accum_out=c_st[:, cs2:cs2 + 1]), "tV")
        tvm = S.op("vector", lambda E: E.reduce_sum(
            c_st[:, cm:cm + 1], src_ap, axis=AX.X), "tV")
        S.wait("vector", "tV", tvm)
        S.op("vector", lambda E: E.tensor_scalar(
            c_st[:, cv2:cv2 + 1], c_st[:, cs2:cs2 + 1], 1.0 / 1024.0, None, ALU.mult))
        t2 = S.op("vector", lambda E: E.tensor_scalar(
            c_st[:, cm:cm + 1], c_st[:, cm:cm + 1], 1.0 / 1024.0, None, ALU.mult), "tV")
        S.wait("vector", "tV", t2)
        t3 = S.op("vector", lambda E: E.scalar_tensor_tensor(
            c_st[:, crs:crs + 1], c_st[:, cm:cm + 1], -1.0, c_st[:, cm:cm + 1],
            ALU.mult, ALU.mult), "tV")
        S.wait("vector", "tV", t3)
        t4 = S.op("vector", lambda E: E.tensor_add(
            c_st[:, cv2:cv2 + 1], c_st[:, cv2:cv2 + 1], c_st[:, crs:crs + 1]), "tV")
        S.wait("scalar", "tV", t4)
        t5 = S.op("scalar", lambda E: E.activation(
            c_st[:, crs:crs + 1], c_st[:, cv2:cv2 + 1], AF.Ln,
            bias=c_btab[:, COL_EPS:COL_EPS + 1]), "tC")
        S.wait("scalar", "tC", t5)
        tc2 = S.op("scalar", lambda E: E.activation(
            c_st[:, crs:crs + 1], c_st[:, crs:crs + 1], AF.Exp,
            scale=c_btab[:, COL_NHALF:COL_NHALF + 1]), "tC")
        S.wait("vector", "tC", tc2)
        tv5 = S.op("vector", lambda E: E.tensor_scalar(
            dst_ap, src_ap, c_st[:, cm:cm + 1], c_st[:, crs:crs + 1],
            ALU.subtract, ALU.mult), "tV")
        return tv5

    prst = {"grp": 0, "kq": 0}

    def proj_512(xsrc, kdst, vdst, qdst):
        gbuf = prst["grp"] % 2
        lnf_off = LNF + gbuf * 4096
        prst["grp"] += 1
        xoff = XB + gbuf * 4096
        S.waitk("sync", ("xb", gbuf))
        vx = S.dma("sync", AR2[:, xoff:xoff + 4096].rearrange("p (s d) -> p s d", d=1024),
                   xsrc.rearrange("(s p) d -> p s d", p=128), "qS")
        S.waitk("vector", ("lnfw", gbuf))
        for sub in range(4):
            lbuf = sub % 2
            loff = LNT + lbuf * 1024
            S.waitk("vector", ("lnt", lbuf))
            S.wait("vector", "qS", vx)
            S.wait("vector", "tV", t_cV)
            tln = ln_block(AR2[:, xoff + sub * 1024:xoff + (sub + 1) * 1024],
                           AR3[:, loff:loff + 1024], AR2[:, JUNK:JUNK + 1024])
            S.waitk("tensor", ("pT",))
            S.wait("tensor", "tV", tln)
            tp = None
            for c in range(8):
                tp = S.op("tensor", lambda E, c=c, o=loff: E.transpose(
                    pT[:, c * 128:(c + 1) * 128],
                    AR3[:, o + c * 128:o + (c + 1) * 128], c_ident[:, :]),
                    "tP" if c == 7 else None)
            last[("lnt", lbuf)] = ("tP", tp)
            S.wait("vector", "tP", tp)
            tv = S.op("vector", lambda E, s=sub, lo=lnf_off: E.tensor_copy(
                AR1[:, lo:lo + 4096].rearrange("p (c n) -> p c n", n=512)[:, :, s * 128:(s + 1) * 128],
                pT[:, :].rearrange("p (c n) -> p c n", n=128)), "tV")
            last[("pT",)] = ("tV", tv)
        v_lnf = S.ctr["tV"]
        last[("xb", gbuf)] = ("tV", S.ctr["tV"])

        def pair_proj(w_off, dsts, scale_col, tag):
            for pr in range(NPAIR):
                kb = prst["kq"] % 2
                prst["kq"] += 1
                ps, pkey = cd_slot()
                S.waitk("tensor", pkey)
                S.wait("tensor", "tV", v_lnf)
                tp = None
                for c in range(8):
                    tp = S.op("tensor", lambda E, c=c, pr=pr, ps=ps, wo=w_off, lo=lnf_off: E.matmul(
                        ps[:, :], AR1[:, wo + (c * 8 + pr) * 128:wo + (c * 8 + pr + 1) * 128],
                        AR1[:, lo + c * 512:lo + (c + 1) * 512],
                        start=(c == 0), stop=(c == 7)), "tP" if c == 7 else None)
                koff = (KBUF if tag == "k" else QBUF) + kb * 512
                if scale_col is None:
                    S.waitk("scalar", (tag + "buf", kb))
                    S.wait("scalar", "tP", tp)
                    tc = S.op("scalar", lambda E, ps=ps, ko=koff: E.activation(
                        AR3[:, ko:ko + 512], ps[:, :], AF.Copy), "tC")
                    last[pkey] = ("tC", tc)
                    S.wait("gpsimd", "tC", tc)
                else:
                    S.waitk("vector", (tag + "buf", kb))
                    S.wait("vector", "tP", tp)
                    tc = S.op("vector", lambda E, ps=ps, ko=koff: E.tensor_scalar(
                        AR3[:, ko:ko + 512], ps[:, :], 0.125, None, ALU.mult), "tV")
                    last[pkey] = ("tV", tc)
                    S.wait("gpsimd", "tV", tc)
                vg = S.dma("gpsimd", dsts[pr], AR3[:, koff:koff + 512], "qG")
                last[(tag + "buf", kb)] = ("qG", vg)

        pair_proj(WK, kdst, None, "k")
        if qdst is not None:
            pair_proj(WQ, qdst, COL_QSC, "q")

        for sub in range(4):
            S.wait("tensor", "tV", v_lnf)
            vb = sub % 2
            voff = VBUF + vb * 1040
            S.waitk("scalar", ("vbuf", vb))
            tcs = None
            for dh in range(2):
                S.waitk("tensor", ("pw",))
                tp = None
                for c in range(8):
                    tp = S.op("tensor", lambda E, c=c, dh=dh, s=sub, lo=lnf_off: E.matmul(
                        pW[:, :],
                        AR1[:, lo + c * 512 + s * 128:lo + c * 512 + (s + 1) * 128],
                        AR1[:, WV + c * 1024 + dh * 512:WV + c * 1024 + (dh + 1) * 512],
                        start=(c == 0), stop=(c == 7)), "tP" if c == 7 else None)
                S.wait("scalar", "tP", tp)
                tcs = S.op("scalar", lambda E, dh=dh, vo=voff: E.activation(
                    AR3[:, vo + dh * 520:vo + (dh + 1) * 520]
                    .rearrange("p (h a) -> p h a", a=65)[:, :, 0:64],
                    pW[:, :].rearrange("p (h a) -> p h a", a=64), AF.Copy), "tC")
                last[("pw",)] = ("tC", tcs)
            S.wait("gpsimd", "tC", tcs)
            vg = S.dma("gpsimd", vdst[sub], AR3[:, voff:voff + 1040], "qG")
            last[("vbuf", vb)] = ("qG", vg)
        last[("lnfw", gbuf)] = ("tP", S.ctr["tP"])

    for b in (range(B) if "P" in PHASES else ()):
        for g in range(16):
            proj_512(ap["x"][b, g * 512:(g + 1) * 512, :],
                     [scr["kscr"][b, pr, :, g * 512:(g + 1) * 512] for pr in range(NPAIR)],
                     [scr["vscr"][b, g * 512 + s * 128:g * 512 + (s + 1) * 128, :] for s in range(4)],
                     None)
    for gs in (range(4) if "O" in PHASES else ()):
        proj_512(ap["xown"][gs],
                 [scr["kown"][gs, :, pr * 512:(pr + 1) * 512] for pr in range(NPAIR)],
                 [scr["vown"][gs, s * 128:(s + 1) * 128, :] for s in range(4)],
                 [scr["qscr"][gs, :, pr * 512:(pr + 1) * 512] for pr in range(NPAIR)])
    t_PO_P = S.ctr["tP"]
    v_PO_G = S.ctr["qG"]

    S.wait("sync", "tP", t_PO_P)
    S.wait("vector", "tP", t_PO_P)
    def wo_loader():
        for pr in range(NPAIR):
            S.dma("sync", AR2[:, XB + pr * 1024:XB + (pr + 1) * 1024],
                  ap["wo"][pr * 128:(pr + 1) * 128, :], "qS")
    stage_cast(wo_loader, WO, 8192)
    t_wo_V = S.ctr["tV"]

    S.wait("gpsimd", "qG", v_PO_G)
    wblk = [0]

    def precast(src, dst):
        wb = wblk[0] % 2
        wblk[0] += 1
        S.waitk("gpsimd", ("wstg", wb))
        dst_st = AR2[:, WSTG + wb * 2048:WSTG + (wb + 1) * 2048]
        if len(src.shape) == 3:
            dst_st = dst_st.rearrange("p (s d) -> p s d", d=src.shape[2])
        vq = S.dma("gpsimd", dst_st, src, "qG")
        S.waitk("gpsimd", ("wcst", wb))
        S.wait("gpsimd", "qG", vq)
        tg = S.op("gpsimd", lambda E, wb=wb: E.tensor_copy(
            AR3[:, WCST + wb * 2048:WCST + (wb + 1) * 2048],
            AR2[:, WSTG + wb * 2048:WSTG + (wb + 1) * 2048]), "tG")
        last[("wstg", wb)] = ("tG", tg)
        src_st = AR3[:, WCST + wb * 2048:WCST + (wb + 1) * 2048]
        if len(dst.shape) == 3:
            src_st = src_st.rearrange("p (s d) -> p s d", d=dst.shape[2])
        vg = S.dma("gpsimd", dst, src_st, "qG")
        last[("wcst", wb)] = ("qG", vg)

    for e in (range(E) if "W" in PHASES else ()):
        for blk in range(8):
            for hf in range(2):
                precast(ap["w1"][e, blk * 128:(blk + 1) * 128, hf * 2048:(hf + 1) * 2048],
                        scr["w1bf"][e, blk * 128:(blk + 1) * 128, hf * 2048:(hf + 1) * 2048])
        for blk in range(16):
            precast(ap["w2"][e, blk * 256:(blk + 1) * 256, :].rearrange("(s p) d -> p s d", p=128),
                    scr["w2bf"][e, blk * 256:(blk + 1) * 256, :].rearrange("(s p) d -> p s d", p=128))
        last[("wbf", e)] = ("qG", S.ctr["qG"])
    v_W_G = S.ctr["qG"]
    t_W_G = S.ctr["tG"]

    ffst = {"w1": 0, "w2": 0, "hho": 0}

    def ffn_gen(g):
        for e in range(E):
            S.waitk("sync", ("wbf", e))
            for hh in range(2):
                for oct_ in range(2):
                    hbase = hh * 16 + oct_ * 8
                    hslot = ffst["hho"] % 2
                    ffst["hho"] += 1
                    hoff = HHO + hslot * 4096
                    for hcl in range(8):
                        hc = hbase + hcl
                        wb = ffst["w1"] % 2
                        ffst["w1"] += 1
                        S.waitk("sync", ("w1s", wb))
                        vq = S.dma("sync", AR1[:, W1S + wb * 1024:W1S + (wb + 1) * 1024]
                                   .rearrange("p (c n) -> p c n", n=128),
                                   scr["w1bf"][e, :, hc * 128:(hc + 1) * 128]
                                   .rearrange("(c p) n -> p c n", p=128), "qC")
                        ps1, pkey = cd_slot()
                        S.waitk("tensor", pkey)
                        S.wait("tensor", "qC", vq)
                        tp = None
                        for c in range(8):
                            tp = S.op("tensor", lambda E, c=c, wb=wb, ps1=ps1: E.matmul(
                                ps1[:, :],
                                AR1[:, W1S + wb * 1024 + c * 128:W1S + wb * 1024 + (c + 1) * 128],
                                AR1[:, XN + c * 512:XN + (c + 1) * 512],
                                start=(c == 0), stop=(c == 7)), "tP" if c == 7 else None)
                        last[("w1s", wb)] = ("tP", tp)
                        S.waitk("scalar", ("hho", hslot, hcl))
                        S.wait("scalar", "tP", tp)
                        tc = S.op("scalar", lambda E, hcl=hcl, hc=hc, e=e, ps1=ps1, ho=hoff: E.activation(
                            AR1[:, ho + hcl * 512:ho + (hcl + 1) * 512], ps1[:, :], AF.Gelu,
                            bias=sb["c_b1t"][:, e * 32 + hc:e * 32 + hc + 1]), "tC")
                        last[pkey] = ("tC", tc)
                        yield
                    t_gel = S.ctr["tC"]
                    for dh in range(2):
                        wb2 = ffst["w2"] % 2
                        ffst["w2"] += 1
                        S.waitk("sync", ("w2q", wb2))
                        vq2 = S.dma("sync", AR1[:, W2Q + wb2 * 4096:W2Q + (wb2 + 1) * 4096]
                                    .rearrange("p (k n) -> p k n", n=512),
                                    scr["w2bf"][e, hbase * 128:(hbase + 8) * 128, dh * 512:(dh + 1) * 512]
                                    .rearrange("(k p) n -> p k n", p=128), "qS")
                        for sub in range(4):
                            S.waitk("tensor", ("pw",))
                            S.wait("tensor", "qS", vq2)
                            S.wait("tensor", "tC", t_gel)
                            tp = None
                            for kk in range(8):
                                tp = S.op("tensor", lambda E, kk=kk, sub=sub, wb2=wb2, ho=hoff: E.matmul(
                                    pW[:, :],
                                    AR1[:, ho + kk * 512 + sub * 128:ho + kk * 512 + (sub + 1) * 128],
                                    AR1[:, W2Q + wb2 * 4096 + kk * 512:W2Q + wb2 * 4096 + (kk + 1) * 512],
                                    start=(kk == 0), stop=(kk == 7)), "tP" if kk == 7 else None)
                            last[("w2q", wb2)] = ("tP", tp)
                            soff = SACC + sub * 1024 + dh * 512
                            cwo = CW + sub * 8 + e
                            S.wait("vector", "tP", tp)
                            tv = S.op("vector", lambda E, so=soff, cwo=cwo: E.scalar_tensor_tensor(
                                AR2[:, so:so + 512], pW[:, :], AR2[:, cwo:cwo + 1],
                                AR2[:, so:so + 512], ALU.mult, ALU.add), "tV")
                            last[("pw",)] = ("tV", tv)
                            yield
                    for hcl in range(8):
                        last[("hho", hslot, hcl)] = ("tP", S.ctr["tP"])
        tfin = S.ctr["tV"]
        S.wait("gpsimd", "tV", tfin)
        vg = S.dma("gpsimd", y[g].rearrange("(s p) d -> p s d", p=128),
                   AR2[:, SACC:SACC + 4096].rearrange("p (s d) -> p s d", d=1024), "qG")
        last[("sacc",)] = ("qG", vg)
        last[("xn",)] = ("tP", S.ctr["tP"])
        ffst["done", g] = True
        while True:
            yield

    def ffn_prep(g):
        S.waitk("sync", ("sacc",))
        S.waitk("sync", ("x1s",))
        vx = S.dma("sync", AR2[:, SACC:SACC + 4096].rearrange("p (s d) -> p s d", d=1024),
                   scr["x1scr"][g].rearrange("(s p) d -> p s d", p=128), "qS")
        for sub in range(4):
            soff = SACC + sub * 1024
            S.wait("vector", "qS", vx)
            tln1 = ln_block(AR2[:, soff:soff + 1024], AR2[:, LNA:LNA + 1024],
                            AR3[:, JKF:JKF + 1024])
            S.wait("vector", "tV", tln1)
            tva = S.op("vector", lambda E, so=soff: E.tensor_add(
                AR2[:, so:so + 1024], AR2[:, so:so + 1024], AR2[:, LNA:LNA + 1024]), "tV")
            S.wait("vector", "tV", tva)
            tln2 = ln_block(AR2[:, LNA:LNA + 1024], AR3[:, LN2N:LN2N + 1024],
                            AR3[:, JKF:JKF + 1024])
            S.waitk("tensor", ("pT",))
            S.wait("tensor", "tV", tln2)
            tp = None
            for c in range(8):
                tp = S.op("tensor", lambda E, c=c: E.transpose(
                    pT[:, c * 128:(c + 1) * 128],
                    AR3[:, LN2N + c * 128:LN2N + (c + 1) * 128], c_ident[:, :]),
                    "tP" if c == 7 else None)
            S.wait("vector", "tP", tp)
            tcx = None
            for c in range(8):
                tcx = S.op("vector", lambda E, c=c, sub=sub: E.tensor_scalar(
                    AR1[:, XN + c * 512 + sub * 128:XN + c * 512 + (sub + 1) * 128],
                    pT[:, c * 128:(c + 1) * 128],
                    sb["c_gb"][:, c:c + 1], sb["c_gb"][:, 8 + c:9 + c],
                    ALU.mult, ALU.add), "tV")
            last[("pT",)] = ("tV", tcx)
            S.waitk("tensor", ("pw",))
            S.wait("tensor", "tV", tcx)
            tpg = None
            for c in range(8):
                tpg = S.op("tensor", lambda E, c=c, sub=sub: E.matmul(
                    pW[:, 0:8],
                    AR1[:, XN + c * 512 + sub * 128:XN + c * 512 + (sub + 1) * 128],
                    sb["c_gw"][:, c * 8:(c + 1) * 8],
                    start=(c == 0), stop=(c == 7)), "tP" if c == 7 else None)
            cwo = CW + sub * 8
            sl = stat_slot[0] % 8
            stat_slot[0] += 1
            cm1, cm1n, cm2, cs2 = sl * 8, sl * 8 + 1, sl * 8 + 2, sl * 8 + 3
            S.wait("vector", "tP", tpg)
            w1_ = S.op("vector", lambda E, cm1=cm1: E.reduce_max(
                c_st[:, cm1:cm1 + 1], pW[:, 0:8], axis=AX.X), "tV")
            S.wait("vector", "tV", w1_)
            w2_ = S.op("vector", lambda E, cm1n=cm1n, cm1=cm1: E.tensor_scalar(
                c_st[:, cm1n:cm1n + 1], c_st[:, cm1:cm1 + 1], -1.0, None, ALU.mult), "tV")
            w3_ = S.op("vector", lambda E: E.tensor_copy(
                AR2[:, XOB:XOB + 8], pW[:, 0:8]), "tV")
            S.wait("vector", "tV", w3_)
            w4_ = S.op("vector", lambda E, cm1=cm1: E.tensor_scalar(
                AR2[:, XOB + 8:XOB + 16], AR2[:, XOB:XOB + 8],
                c_st[:, cm1:cm1 + 1], None, ALU.is_equal), "tV")
            S.wait("vector", "tV", w4_)
            w5_ = S.op("vector", lambda E: E.scalar_tensor_tensor(
                AR2[:, XOB + 8:XOB + 16], AR2[:, XOB + 8:XOB + 16], -1e30,
                AR2[:, XOB:XOB + 8], ALU.mult, ALU.add), "tV")
            S.wait("vector", "tV", w5_)
            w6_ = S.op("vector", lambda E, cm2=cm2: E.reduce_max(
                c_st[:, cm2:cm2 + 1], AR2[:, XOB + 8:XOB + 16], axis=AX.X), "tV")
            S.wait("vector", "tV", w6_)
            tvm = S.op("vector", lambda E, cm2=cm2: E.tensor_scalar(
                AR2[:, XOB + 8:XOB + 16], AR2[:, XOB:XOB + 8],
                c_st[:, cm2:cm2 + 1], None, ALU.is_ge), "tV")
            S.wait("scalar", "tV", max(tvm, w2_))
            tce = S.op("scalar", lambda E, cm1n=cm1n: E.activation(
                AR2[:, XOB + 16:XOB + 24], AR2[:, XOB:XOB + 8], AF.Exp,
                bias=c_st[:, cm1n:cm1n + 1]), "tC")
            S.wait("vector", "tC", tce)
            w7_ = S.op("vector", lambda E: E.tensor_mul(
                AR2[:, XOB + 16:XOB + 24], AR2[:, XOB + 16:XOB + 24],
                AR2[:, XOB + 8:XOB + 16]), "tV")
            S.wait("vector", "tV", w7_)
            w8_ = S.op("vector", lambda E, cs2=cs2: E.reduce_sum(
                c_st[:, cs2:cs2 + 1], AR2[:, XOB + 16:XOB + 24], axis=AX.X), "tV")
            S.wait("vector", "tV", w8_)
            w9_ = S.op("vector", lambda E, cs2=cs2: E.reciprocal(
                c_st[:, cs2 + 1:cs2 + 2], c_st[:, cs2:cs2 + 1]), "tV")
            S.wait("vector", "tV", w9_)
            tvc = S.op("vector", lambda E, cwo=cwo, cs2=cs2: E.tensor_scalar(
                AR2[:, cwo:cwo + 8], AR2[:, XOB + 16:XOB + 24],
                c_st[:, cs2 + 1:cs2 + 2], None, ALU.mult), "tV")
            S.wait("vector", "tV", tvc)
            tvb = S.op("vector", lambda E, cwo=cwo: E.tensor_copy(
                AR3[:, CWB:CWB + 8], AR2[:, cwo:cwo + 8]), "tV")
            S.waitk("tensor", ("pcwt",))
            S.wait("tensor", "tV", tvb)
            tpt = S.op("tensor", lambda E: E.transpose(
                pT[0:8, 0:128], AR3[:, CWB:CWB + 8], c_ident[:, :]), "tP")
            S.wait("vector", "tP", tpt)
            tvt = S.op("vector", lambda E, sub=sub: E.tensor_copy(
                sb["c_cwt"][0:8, sub * 128:(sub + 1) * 128], pT[0:8, 0:128]), "tV")
            last[("pcwt",)] = ("tV", tvt)
            S.wait("tensor", "tV", tvt)
            for dh in range(2):
                S.waitk("tensor", ("pw",))
                tpb = S.op("tensor", lambda E, dh=dh, sub=sub: E.matmul(
                    pW[:, :],
                    sb["c_cwt"][0:8, sub * 128:(sub + 1) * 128],
                    sb["c_b2"][0:8, dh * 512:(dh + 1) * 512],
                    start=True, stop=True), "tP")
                S.wait("vector", "tP", tpb)
                tvb2 = S.op("vector", lambda E, dh=dh, so=soff: E.tensor_add(
                    AR2[:, so + dh * 512:so + (dh + 1) * 512],
                    AR2[:, so + dh * 512:so + (dh + 1) * 512], pW[:, :]), "tV")
                last[("pw",)] = ("tV", tvb2)
        last[("xob", 0)] = ("tV", S.ctr["tV"])
        last[("xob", 1)] = ("tV", S.ctr["tV"])

    for k in (("qsl",), ("kvst",), ("xn",)):
        last[k] = ("tP", t_PO_P)
    aabc = [0]
    ofirst = [True] * 16
    pend = {"gen": None}

    def pump(n):
        if pend["gen"] is not None:
            for _ in range(n):
                next(pend["gen"])

    def drain(g):
        while pend["gen"] is not None and not ffst.get(("done", g)):
            next(pend["gen"])

    def attn_group(s, grp, diag):
        for pr in range(NPAIR):
            pump(2)
            ab = aabc[0] % 2
            aabc[0] += 1
            aoff = AAB + ab * 4096
            S.waitk("scalar", ("aab", ab))
            texp = []
            for ih in range(2):
                S.waitk("tensor", ("sps",))
                for i in (2 * ih, 2 * ih + 1):
                    po = (i % 2) * 1024
                    for h in range(2):
                        inc = (h == 1) and not diag
                        S.op("tensor", lambda E, h=h, i=i, pr=pr, po=po: E.matmul(
                            pS[:, po + h * 512:po + (h + 1) * 512],
                            AR1[h * 64:(h + 1) * 64, KST + pr * 512 + i * 128:KST + pr * 512 + (i + 1) * 128],
                            AR1[h * 64:(h + 1) * 64, QSL + pr * 512:QSL + (pr + 1) * 512],
                            start=True, stop=(not diag), tile_position=(h * 64, 0)),
                            "tP" if inc else None)
                        if diag:
                            S.op("tensor", lambda E, h=h, i=i, po=po: E.matmul(
                                pS[:, po + h * 512:po + (h + 1) * 512], c_ident[:, :],
                                c_maskp[:, i * 512:(i + 1) * 512],
                                start=False, stop=True), "tP" if h == 1 else None)
                tp = S.ctr["tP"]
                S.wait("scalar", "tP", tp)
                bcol = 96 if diag else (grp if s == 0 else 24 + grp)
                tc = S.op("scalar", lambda E, ih=ih, ao=aoff, bc=bcol: E.activation(
                    AR1[:, ao + ih * 2048:ao + (ih + 1) * 2048], pS[:, :], AF.Exp,
                    bias=c_btab[:, bc:bc + 1]), "tC")
                texp.append(tc)
                last[("sps",)] = ("tC", tc)
            for h in range(2):
                psz, pkey = cd_slot()
                S.waitk("tensor", pkey)
                S.wait("tensor", "tC", texp[-1])
                tp_av = None
                for i in range(GG):
                    tp_av = S.op("tensor", lambda E, i=i, h=h, pr=pr, psz=psz, ao=aoff: E.matmul(
                        psz[0:65, :],
                        AR1[:, VST + i * 1040 + (2 * pr + h) * 65:VST + i * 1040 + (2 * pr + h) * 65 + 65],
                        AR1[:, ao + i * 1024 + h * 512:ao + i * 1024 + (h + 1) * 512],
                        start=(i == 0), stop=(i == GG - 1)), "tP" if i == GG - 1 else None)
                slot = pr * 2 + h
                ooff = OACC + slot * 512
                S.wait("vector", "tP", tp_av)
                if ofirst[slot]:
                    tv = S.op("vector", lambda E, o=ooff, psz=psz: E.tensor_copy(
                        AR2[0:65, o:o + 512], psz[0:65, :]), "tV")
                    ofirst[slot] = False
                else:
                    tv = S.op("vector", lambda E, o=ooff, psz=psz: E.tensor_add(
                        AR2[0:65, o:o + 512], AR2[0:65, o:o + 512], psz[0:65, :]), "tV")
                last[pkey] = ("tV", tv)
            last[("aab", ab)] = ("tP", S.ctr["tP"])

    for bs in (range(4) if "A" in PHASES else ()):
        b, s = bs // 2, bs % 2
        S.waitk("sync", ("qsl",))
        S.wait("sync", "qG", v_PO_G)
        vq = S.dma("sync", AR1[:, QSL:QSL + 4096], scr["qscr"][bs], "qS")
        S.wait("tensor", "qS", vq)
        S.wait("tensor", "tV", t_wo_V)
        for i in range(16):
            ofirst[i] = True
        ngrp = CAP[s] // GG
        for grp in range(ngrp + 1):
            diag = grp == ngrp
            S.waitk("sync", ("kvst",))
            if diag:
                S.dma("sync", AR1[:, KST:KST + 4096], scr["kown"][bs], "qS")
                vv = S.dma("sync", AR1[:, VST:VST + 4160].rearrange("p (i d) -> p i d", d=1040),
                           scr["vown"][bs].rearrange("(i p) d -> p i d", p=128), "qS")
            else:
                S.dma("sync", AR1[:, KST:KST + 4096].rearrange("p (r t) -> p r t", t=512),
                      scr["kscr"][b, :, :, grp * 512:(grp + 1) * 512]
                      .rearrange("r p t -> p r t"), "qS")
                vv = S.dma("sync", AR1[:, VST:VST + 4160].rearrange("p (i d) -> p i d", d=1040),
                           scr["vscr"][b, grp * 512:(grp + 1) * 512, :]
                           .rearrange("(i p) d -> p i d", p=128), "qS")
            S.wait("tensor", "qS", vv)
            attn_group(s, grp, diag)
            last[("kvst",)] = ("tP", S.ctr["tP"])
            pump(2 if s == 0 else 14)

        for pr in range(NPAIR):
            for h in range(2):
                slot = pr * 2 + h
                ooff = OACC + slot * 512
                S.waitk("scalar", ("zrw",))
                z1 = S.op("scalar", lambda E, o=ooff: E.activation(
                    AR2[0:1, LNA:LNA + 512], AR2[64:65, o:o + 512], AF.Ln), "tC")
                S.wait("scalar", "tC", z1)
                z1b = S.op("scalar", lambda E: E.activation(
                    AR2[0:1, LNA:LNA + 512], AR2[0:1, LNA:LNA + 512], AF.Exp,
                    scale=c_btab[0:1, 95:96]), "tC")
                S.wait("vector", "tC", z1b)
                z2 = S.op("vector", lambda E: E.tensor_copy(
                    AR3[0:1, JKF:JKF + 512], AR2[0:1, LNA:LNA + 512]), "tV")
                S.wait("vector", "tV", z2)
                z3 = S.op("vector", lambda E: E.tensor_sub(
                    AR2[0:1, LNA:LNA + 512], AR2[0:1, LNA:LNA + 512], AR3[0:1, JKF:JKF + 512]), "tV")
                S.wait("vector", "tV", z3)
                tvr = S.op("vector", lambda E: E.tensor_copy(
                    AR3[0:1, JKF + 512:JKF + 1024], AR2[0:1, LNA:LNA + 512]), "tV")
                psz, pkey = cd_slot()
                S.waitk("tensor", pkey)
                S.wait("tensor", "tV", tvr)
                S.op("tensor", lambda E, psz=psz: E.matmul(
                    psz[0:64, :], sb["c_onb"][0:1, :], AR3[0:1, JKF:JKF + 512],
                    start=True, stop=False), None)
                tpb = S.op("tensor", lambda E, psz=psz: E.matmul(
                    psz[0:64, :], sb["c_onb"][0:1, :], AR3[0:1, JKF + 512:JKF + 1024],
                    start=False, stop=True), "tP")
                last[("zrw",)] = ("tP", tpb)
                S.wait("vector", "tP", tpb)
                tvn = S.op("vector", lambda E, o=ooff, h=h, pr=pr, psz=psz: E.tensor_mul(
                    AR1[h * 64:(h + 1) * 64, ONORM + pr * 512:ONORM + (pr + 1) * 512],
                    AR2[0:64, o:o + 512], psz[0:64, :]), "tV")
                last[pkey] = ("tV", tvn)
            pump(6)
        v_on = S.ctr["tV"]

        for sub in range(4):
            xb2 = sub % 2
            S.waitk("sync", ("xob", xb2))
            vx = S.dma("sync", AR2[:, XOB + xb2 * 1024:XOB + (xb2 + 1) * 1024],
                       ap["xown"][bs, sub * 128:(sub + 1) * 128, :], "qS")
            tvs = None
            for dh in range(2):
                S.waitk("tensor", ("pw",))
                S.wait("tensor", "tV", v_on)
                tp = None
                for pr in range(NPAIR):
                    tp = S.op("tensor", lambda E, dh=dh, pr=pr, sub=sub: E.matmul(
                        pW[:, :],
                        AR1[:, ONORM + pr * 512 + sub * 128:ONORM + pr * 512 + (sub + 1) * 128],
                        AR1[:, WO + pr * 1024 + dh * 512:WO + pr * 1024 + (dh + 1) * 512],
                        start=(pr == 0), stop=(pr == 7)), "tP" if pr == 7 else None)
                x1b = sub % 2
                S.waitk("vector", ("x1b", x1b))
                S.wait("vector", "tP", tp)
                S.wait("vector", "qS", vx)
                tvs = S.op("vector", lambda E, dh=dh, x1b=x1b, xb2=xb2: E.tensor_add(
                    AR2[:, X1B + x1b * 1024 + dh * 512:X1B + x1b * 1024 + (dh + 1) * 512],
                    pW[:, :],
                    AR2[:, XOB + xb2 * 1024 + dh * 512:XOB + xb2 * 1024 + (dh + 1) * 512]), "tV")
                last[("pw",)] = ("tV", tvs)
            last[("xob", xb2)] = ("tV", tvs)
            S.wait("scalar", "tV", tvs)
            vg = S.dma("scalar", scr["x1scr"][bs, sub * 128:(sub + 1) * 128, :],
                       AR2[:, X1B + (sub % 2) * 1024:X1B + (sub % 2 + 1) * 1024], "qC")
            last[("x1b", sub % 2)] = ("qC", vg)
            pump(6)
        last[("qsl",)] = ("tP", S.ctr["tP"])
        last[("kvst",)] = ("tP", S.ctr["tP"])
        last[("x1s",)] = ("qC", S.ctr["qC"])

        if "F" in PHASES:
            if pend["gen"] is not None:
                drain(bs - 1)
            S.wait("sync", "qC", S.ctr["qC"])
            ffn_prep(bs)
            pend["gen"] = ffn_gen(bs)
    if "F" in PHASES and pend["gen"] is not None:
        drain(3)

    if taps:
        S.wait("gpsimd", "qG", S.ctr["qG"])
        S.wait("gpsimd", "qC", S.ctr["qC"])
        S.wait("gpsimd", "tV", S.ctr["tV"])
        S.dma("gpsimd", taps["t_kscr"], scr["kscr"][0], "qG")
        S.dma("gpsimd", taps["t_vscr"], scr["vscr"][0], "qG")
        S.dma("gpsimd", taps["t_qscr"], scr["qscr"][:, :, :], "qG")
        S.dma("gpsimd", taps["t_x1"], scr["x1scr"][:, :, :], "qG")
        S.dma("gpsimd", taps["t_w1bf"], scr["w1bf"][0], "qG")
        S.dma("gpsimd", taps["t_w2bf"], scr["w2bf"][0], "qG")
    S.wait("gpsimd", "qG", S.ctr["qG"])


def _host_constants(j):
    ident = np.eye(128, dtype=np.float32)
    maskp = np.zeros((4, 128, 512), np.float32)
    for i in range(4):
        kt = i * 128 + np.arange(128)[:, None]
        q = np.arange(512)[None, :]
        maskp[i] = np.where(kt <= q, 0.0, NEGBIG)
    btab = np.zeros((128, 100), np.float32)
    len0 = 4 * (16 - j)
    len1 = 4 * (j + 1)
    for gg in range(16):
        btab[:, gg] = -SHIFT if gg * 4 < len0 - 4 else NEGBIG
    for gg in range(8):
        btab[:, 24 + gg] = -SHIFT if gg * 4 < len1 - 4 else NEGBIG
    btab[:, 95] = -1.0
    btab[:, 96] = -SHIFT
    btab[:, 97] = 1e-5
    btab[:, 98] = -0.5
    btab[:, 99] = 0.125
    return ident, maskp, btab


def kernel(L0_attn_x, L0_attn_w_q, L0_attn_w_k, L0_attn_w_v, L0_attn_w_o,
           L0_ffn_gate_w, L0_ffn_w1, L0_ffn_b1, L0_ffn_w2, L0_ffn_b2,
           L0_ffn_gamma, L0_ffn_beta):
    x = np.ascontiguousarray(np.asarray(L0_attn_x, np.float32))
    shared = {
        "x": x,
        "wq": np.ascontiguousarray(np.asarray(L0_attn_w_q, np.float32)),
        "wk": np.ascontiguousarray(np.asarray(L0_attn_w_k, np.float32)),
        "wv": np.ascontiguousarray(np.asarray(L0_attn_w_v, np.float32)),
        "wo": np.ascontiguousarray(np.asarray(L0_attn_w_o, np.float32)),
        "gw": np.ascontiguousarray(np.asarray(L0_ffn_gate_w, np.float32)),
        "w1": np.ascontiguousarray(np.asarray(L0_ffn_w1, np.float32)),
        "b1": np.ascontiguousarray(np.asarray(L0_ffn_b1, np.float32)),
        "gbc": np.ascontiguousarray(np.concatenate(
            [np.asarray(L0_ffn_gamma, np.float32).reshape(8, 128).T,
             np.asarray(L0_ffn_beta, np.float32).reshape(8, 128).T], axis=1)),
        "b1t": np.ascontiguousarray(
            np.asarray(L0_ffn_b1, np.float32).reshape(E * 32, 128).T),
        "w2": np.ascontiguousarray(np.asarray(L0_ffn_w2, np.float32)),
        "b2": np.ascontiguousarray(np.asarray(L0_ffn_b2, np.float32)),
    }
    in_maps = []
    for j in range(NCORE):
        ident, maskp, btab = _host_constants(j)
        rows = (15 - j, j)
        xo = np.stack([x[b, r * 512:(r + 1) * 512, :] for b in range(B) for r in rows])
        m = dict(shared)
        m["xown"] = np.ascontiguousarray(xo)
        m["ident"] = ident
        m["maskp"] = np.ascontiguousarray(maskp)
        m["btab"] = btab
        in_maps.append(m)

    nc = build_program()
    res = run_bass_kernel_spmd(nc, in_maps, core_ids=list(range(NCORE)))
    out = np.empty((B, T, D), np.float32)
    for j in range(NCORE):
        yj = np.asarray(res.results[j]["y"], np.float32)
        rows = (15 - j, j)
        for b in range(B):
            for si, r in enumerate(rows):
                out[b, r * 512:(r + 1) * 512, :] = yj[b * 2 + si]
    return out


# version 09
# speedup vs baseline: 1.3806x; 1.0257x over previous
import numpy as np
from contextlib import ExitStack

import concourse.bass as bass
import concourse.mybir as mybir
from concourse.bass_utils import run_bass_kernel_spmd

F32 = mybir.dt.float32
BF16 = mybir.dt.bfloat16
AF = mybir.ActivationFunctionType
ALU = mybir.AluOpType
AX = mybir.AxisListType

B, T, D = 2, 8192, 1024
NPAIR, E = 8, 8
NCORE = 8
PHASES = "CPOWAF"
DEBUG_TAPS = False
SHIFT = 32.0
NEGBIG = -30000.0
CAP = (64, 32)
GG = 4

ENGS = ("sync", "vector", "scalar", "tensor", "gpsimd")

WQ, WK, WV, LNF = 0, 8192, 16384, 24576
WO = 20608
QSL, KST, VST, AAB = 0, 4096, 8192, 12352
ONORM = 4096
XN, HHO, W1S, W2Q = 28800, 32896, 41088, 43136
XB, JUNK = 0, 8192
OACC = 0
SACC, WSTG = 8192, 12288
X1B, XOB, LNA, CW = 16384, 18432, 20480, 21504
LNT, KBUF, VBUF, QBUF = 0, 2048, 3072, 5152
WCST = 0
LN2N, JKF, CWB = 4096, 5120, 6144

COL_EPS, COL_NHALF, COL_QSC = 97, 98, 99


class Sched:
    def __init__(self):
        self.ops = {e: [] for e in ENGS}
        self.ctr = {s: 0 for s in ("qS", "qG", "qC", "tP", "tC", "tV", "tG")}
        self.last = {}

    def on(self, eng, fn):
        self.ops[eng].append(fn)

    def wait(self, eng, sem, val):
        if val is None or val <= 0:
            return
        self.on(eng, lambda E, s=sem, v=val: E.wait_ge(E._sems[s], v))

    def waitk(self, eng, key):
        tok = self.last.get(key)
        if tok is not None:
            self.wait(eng, tok[0], tok[1])

    def dma(self, eng, out, in_, sem):
        self.ctr[sem] += 16
        v = self.ctr[sem]
        self.on(eng, lambda E, o=out, i=in_, s=sem: E.dma_start(out=o, in_=i).then_inc(E._sems[s], 16))
        return v

    def op(self, eng, fn, sem=None):
        if sem is None:
            self.on(eng, fn)
            return None
        self.ctr[sem] += 1
        v = self.ctr[sem]
        def run(E, f=fn, s=sem):
            f(E).then_inc(E._sems[s], 1)
        self.on(eng, run)
        return v


def build_program():
    nc = bass.Bass("TRN2", target_bir_lowering=False, debug=False)
    ap = {}
    for name, shape in (("x", [B, T, D]), ("xown", [4, 512, D]),
                        ("wq", [16, D, 64]), ("wk", [16, D, 64]), ("wv", [16, D, 64]),
                        ("wo", [D, D]), ("gw", [D, E]),
                        ("w1", [E, D, 4096]), ("b1", [E, 4096]),
                        ("w2", [E, 4096, D]), ("b2", [E, D]),
                        ("gbc", [128, 16]), ("b1t", [128, 256]),
                        ("ident", [128, 128]), ("maskp", [4, 128, 512]),
                        ("btab", [128, 100])):
        ap[name] = nc.dram_tensor(name, shape, F32, kind="ExternalInput").ap()
    y = nc.dram_tensor("y", [4, 512, D], F32, kind="ExternalOutput").ap()
    taps = {}
    if DEBUG_TAPS:
        taps["t_kscr"] = nc.dram_tensor("t_kscr", [NPAIR, 128, T], BF16, kind="ExternalOutput").ap()
        taps["t_vscr"] = nc.dram_tensor("t_vscr", [T, 1040], BF16, kind="ExternalOutput").ap()
        taps["t_qscr"] = nc.dram_tensor("t_qscr", [4, 128, 4096], BF16, kind="ExternalOutput").ap()
        taps["t_x1"] = nc.dram_tensor("t_x1", [4, 512, D], F32, kind="ExternalOutput").ap()
        taps["t_w1bf"] = nc.dram_tensor("t_w1bf", [D, 4096], BF16, kind="ExternalOutput").ap()
        taps["t_w2bf"] = nc.dram_tensor("t_w2bf", [4096, D], BF16, kind="ExternalOutput").ap()

    scr = {
        "kscr": nc.dram_tensor("kscr", [B, NPAIR, 128, T], BF16, kind="Internal").ap(),
        "vscr": nc.dram_tensor("vscr", [B, T, 1040], BF16, kind="Internal").ap(),
        "qscr": nc.dram_tensor("qscr", [4, 128, 4096], BF16, kind="Internal").ap(),
        "kown": nc.dram_tensor("kown", [4, 128, 4096], BF16, kind="Internal").ap(),
        "vown": nc.dram_tensor("vown", [4, 512, 1040], BF16, kind="Internal").ap(),
        "x1scr": nc.dram_tensor("x1scr", [4, 512, D], F32, kind="Internal").ap(),
        "w1bf": nc.dram_tensor("w1bf", [E, D, 4096], BF16, kind="Internal").ap(),
        "w2bf": nc.dram_tensor("w2bf", [E, 4096, D], BF16, kind="Internal").ap(),
    }

    with ExitStack() as stk:
        En = stk.enter_context
        sb = {
            "AR1": En(nc.sbuf_tensor("AR1", [128, 51456], BF16)),
            "AR2": En(nc.sbuf_tensor("AR2", [128, 21760], F32)),
            "AR3": En(nc.sbuf_tensor("AR3", [128, 6272], BF16)),
            "c_ident": En(nc.sbuf_tensor("c_ident", [128, 128], BF16)),
            "c_maskp": En(nc.sbuf_tensor("c_maskp", [128, 2048], BF16)),
            "c_btab": En(nc.sbuf_tensor("c_btab", [128, 100], F32)),
            "c_gb": En(nc.sbuf_tensor("c_gb", [128, 16], F32)),
            "c_b1t": En(nc.sbuf_tensor("c_b1t", [128, 256], F32)),
            "c_b2": En(nc.sbuf_tensor("c_b2", [8, 1024], BF16)),
            "c_gw": En(nc.sbuf_tensor("c_gw", [128, 64], BF16)),
            "c_st": En(nc.sbuf_tensor("c_st", [128, 64], F32)),
            "c_onb": En(nc.sbuf_tensor("c_onb", [1, 64], BF16)),
            "c_cwt": En(nc.sbuf_tensor("c_cwt", [8, 512], BF16)),
            "c_f128": En(nc.sbuf_tensor("c_f128", [128, 128], F32)),
            "pS": En(nc.psum_tensor("pS", [128, 2048], F32)),
            "pC": En(nc.psum_tensor("pC", [128, 512], F32)),
            "pD": En(nc.psum_tensor("pD", [128, 512], F32)),
            "pW": En(nc.psum_tensor("pW", [128, 512], F32)),
            "pT": En(nc.psum_tensor("pT", [128, 1024], BF16)),
        }
        sems = {s: En(nc.semaphore(s)) for s in ("qS", "qG", "qC", "tP", "tC", "tV", "tG")}

        S = Sched()
        emit_all(S, ap, y, scr, sb, taps)

        block = En(nc.Block())
        def section(eng_name):
            def body(E):
                E._sems = sems
                for fn in S.ops[eng_name]:
                    fn(E)
            return body
        block.sync(section("sync"))
        block.vector(section("vector"))
        block.scalar(section("scalar"))
        block.tensor(section("tensor"))
        block.gpsimd(section("gpsimd"))
    return nc


def emit_all(S, ap, y, scr, sb, taps=None):
    AR1, AR2, AR3 = sb["AR1"], sb["AR2"], sb["AR3"]
    pS, pC, pD, pW, pT = sb["pS"], sb["pC"], sb["pD"], sb["pW"], sb["pT"]
    c_btab, c_st, c_ident, c_maskp = sb["c_btab"], sb["c_st"], sb["c_ident"], sb["c_maskp"]
    last = S.last

    cdp = [0]
    def cd_slot():
        pp = cdp[0] % 2
        cdp[0] += 1
        return (pC, ("pcd", 0)) if pp == 0 else (pD, ("pcd", 1))

    S.dma("sync", c_btab[:, :], ap["btab"], "qS")
    S.dma("sync", sb["c_f128"][:, :], ap["ident"], "qS")
    for mi in range(4):
        S.dma("sync", AR2[:, JUNK + mi * 512:JUNK + (mi + 1) * 512], ap["maskp"][mi], "qS")
    S.dma("sync", sb["c_gb"][:, :], ap["gbc"], "qS")
    S.dma("sync", sb["c_b1t"][:, :], ap["b1t"], "qS")
    S.dma("sync", AR2[0:8, JUNK + 2048:JUNK + 3072], ap["b2"], "qS")
    for c in range(8):
        S.dma("sync", AR2[:, JUNK + 3072 + c * 8:JUNK + 3072 + (c + 1) * 8],
              ap["gw"][c * 128:(c + 1) * 128, :], "qS")
    v_ld = S.ctr["qS"]

    S.wait("vector", "qS", v_ld)
    S.op("vector", lambda E: E.tensor_copy(c_ident[:, :], sb["c_f128"][:, :]))
    S.op("vector", lambda E: E.tensor_copy(c_maskp[:, :], AR2[:, JUNK:JUNK + 2048]))
    S.op("vector", lambda E: E.tensor_copy(sb["c_b2"][0:8, :], AR2[0:8, JUNK + 2048:JUNK + 3072]))
    S.op("vector", lambda E: E.tensor_copy(sb["c_gw"][:, :], AR2[:, JUNK + 3072:JUNK + 3136]))
    S.op("vector", lambda E: E.memset(sb["c_onb"][0:1, :], 1.0))
    for vb in range(2):
        off = VBUF + vb * 1040
        S.op("vector", lambda E, o=off: E.memset(
            AR3[:, o:o + 1040].rearrange("p (h a) -> p h a", a=65)[:, :, 64:65], 1.0))
    v_cast0 = S.op("vector", lambda E: E.engine_nop(), "tV")
    t_cV = v_cast0

    def stage_cast(loader, dst_off, ncols):
        loader()
        vq = S.ctr["qS"]
        S.wait("vector", "qS", vq)
        tv = S.op("vector", lambda E, o=dst_off, n=ncols: E.tensor_copy(
            AR1[:, o:o + n], AR2[:, XB:XB + n]), "tV")
        S.wait("sync", "tV", tv)
        return tv

    def pairs_loader(name):
        def go():
            for c in range(8):
                for pr in range(NPAIR):
                    blk = c * 8 + pr
                    src = ap[name][2 * pr:2 * pr + 2, c * 128:(c + 1) * 128, :] \
                        .rearrange("h p a -> p h a")
                    dst = AR2[:, XB + blk * 128:XB + (blk + 1) * 128] \
                        .rearrange("p (h a) -> p h a", a=64)
                    S.dma("sync", dst, src, "qS")
        return go
    stage_cast(pairs_loader("wq"), WQ, 8192)
    stage_cast(pairs_loader("wk"), WK, 8192)
    def wv_loader():
        for c in range(8):
            src = ap["wv"][:, c * 128:(c + 1) * 128, :].rearrange("h p a -> p h a")
            dst = AR2[:, XB + c * 1024:XB + (c + 1) * 1024] \
                .rearrange("p (h a) -> p h a", a=64)
            S.dma("sync", dst, src, "qS")
    stage_cast(wv_loader, WV, 8192)

    stat_slot = [0]

    def ln_block(src_ap, dst_ap, junk_ap):
        sl = stat_slot[0] % 8
        stat_slot[0] += 1
        cm, cs2, cv2, crs = sl * 8, sl * 8 + 1, sl * 8 + 2, sl * 8 + 3
        tv0 = S.op("vector", lambda E: E.scalar_tensor_tensor(
            junk_ap, src_ap, 1.0, src_ap, ALU.mult, ALU.mult,
            accum_out=c_st[:, cs2:cs2 + 1]), "tV")
        tvm = S.op("vector", lambda E: E.reduce_sum(
            c_st[:, cm:cm + 1], src_ap, axis=AX.X), "tV")
        S.wait("vector", "tV", tvm)
        S.op("vector", lambda E: E.tensor_scalar(
            c_st[:, cv2:cv2 + 1], c_st[:, cs2:cs2 + 1], 1.0 / 1024.0, None, ALU.mult))
        t2 = S.op("vector", lambda E: E.tensor_scalar(
            c_st[:, cm:cm + 1], c_st[:, cm:cm + 1], 1.0 / 1024.0, None, ALU.mult), "tV")
        S.wait("vector", "tV", t2)
        t3 = S.op("vector", lambda E: E.scalar_tensor_tensor(
            c_st[:, crs:crs + 1], c_st[:, cm:cm + 1], -1.0, c_st[:, cm:cm + 1],
            ALU.mult, ALU.mult), "tV")
        S.wait("vector", "tV", t3)
        t4 = S.op("vector", lambda E: E.tensor_add(
            c_st[:, cv2:cv2 + 1], c_st[:, cv2:cv2 + 1], c_st[:, crs:crs + 1]), "tV")
        S.wait("scalar", "tV", t4)
        t5 = S.op("scalar", lambda E: E.activation(
            c_st[:, crs:crs + 1], c_st[:, cv2:cv2 + 1], AF.Ln,
            bias=c_btab[:, COL_EPS:COL_EPS + 1]), "tC")
        S.wait("scalar", "tC", t5)
        tc2 = S.op("scalar", lambda E: E.activation(
            c_st[:, crs:crs + 1], c_st[:, crs:crs + 1], AF.Exp,
            scale=c_btab[:, COL_NHALF:COL_NHALF + 1]), "tC")
        S.wait("vector", "tC", tc2)
        tv5 = S.op("vector", lambda E: E.tensor_scalar(
            dst_ap, src_ap, c_st[:, cm:cm + 1], c_st[:, crs:crs + 1],
            ALU.subtract, ALU.mult), "tV")
        return tv5

    prst = {"grp": 0, "kq": 0}

    def proj_512(xsrc, kdst, vdst, qdst):
        gbuf = prst["grp"] % 2
        lnf_off = LNF + gbuf * 4096
        prst["grp"] += 1
        xoff = XB + gbuf * 4096
        S.waitk("sync", ("xb", gbuf))
        vx = S.dma("sync", AR2[:, xoff:xoff + 4096].rearrange("p (s d) -> p s d", d=1024),
                   xsrc.rearrange("(s p) d -> p s d", p=128), "qS")
        S.waitk("vector", ("lnfw", gbuf))
        for sub in range(4):
            lbuf = sub % 2
            loff = LNT + lbuf * 1024
            S.waitk("vector", ("lnt", lbuf))
            S.wait("vector", "qS", vx)
            S.wait("vector", "tV", t_cV)
            tln = ln_block(AR2[:, xoff + sub * 1024:xoff + (sub + 1) * 1024],
                           AR3[:, loff:loff + 1024], AR2[:, JUNK:JUNK + 1024])
            S.waitk("tensor", ("pT",))
            S.wait("tensor", "tV", tln)
            tp = None
            for c in range(8):
                tp = S.op("tensor", lambda E, c=c, o=loff: E.transpose(
                    pT[:, c * 128:(c + 1) * 128],
                    AR3[:, o + c * 128:o + (c + 1) * 128], c_ident[:, :]),
                    "tP" if c == 7 else None)
            last[("lnt", lbuf)] = ("tP", tp)
            S.wait("vector", "tP", tp)
            tv = S.op("vector", lambda E, s=sub, lo=lnf_off: E.tensor_copy(
                AR1[:, lo:lo + 4096].rearrange("p (c n) -> p c n", n=512)[:, :, s * 128:(s + 1) * 128],
                pT[:, :].rearrange("p (c n) -> p c n", n=128)), "tV")
            last[("pT",)] = ("tV", tv)
        v_lnf = S.ctr["tV"]
        last[("xb", gbuf)] = ("tV", S.ctr["tV"])

        def pair_proj(w_off, dsts, scale_col, tag):
            for pr in range(NPAIR):
                kb = prst["kq"] % 2
                prst["kq"] += 1
                ps, pkey = cd_slot()
                S.waitk("tensor", pkey)
                S.wait("tensor", "tV", v_lnf)
                tp = None
                for c in range(8):
                    tp = S.op("tensor", lambda E, c=c, pr=pr, ps=ps, wo=w_off, lo=lnf_off: E.matmul(
                        ps[:, :], AR1[:, wo + (c * 8 + pr) * 128:wo + (c * 8 + pr + 1) * 128],
                        AR1[:, lo + c * 512:lo + (c + 1) * 512],
                        start=(c == 0), stop=(c == 7)), "tP" if c == 7 else None)
                koff = (KBUF if tag == "k" else QBUF) + kb * 512
                if scale_col is None:
                    S.waitk("scalar", (tag + "buf", kb))
                    S.wait("scalar", "tP", tp)
                    tc = S.op("scalar", lambda E, ps=ps, ko=koff: E.activation(
                        AR3[:, ko:ko + 512], ps[:, :], AF.Copy), "tC")
                    last[pkey] = ("tC", tc)
                    S.wait("gpsimd", "tC", tc)
                else:
                    S.waitk("vector", (tag + "buf", kb))
                    S.wait("vector", "tP", tp)
                    tc = S.op("vector", lambda E, ps=ps, ko=koff: E.tensor_scalar(
                        AR3[:, ko:ko + 512], ps[:, :], 0.125, None, ALU.mult), "tV")
                    last[pkey] = ("tV", tc)
                    S.wait("gpsimd", "tV", tc)
                vg = S.dma("gpsimd", dsts[pr], AR3[:, koff:koff + 512], "qG")
                last[(tag + "buf", kb)] = ("qG", vg)

        pair_proj(WK, kdst, None, "k")
        if qdst is not None:
            pair_proj(WQ, qdst, COL_QSC, "q")

        for sub in range(4):
            S.wait("tensor", "tV", v_lnf)
            vb = sub % 2
            voff = VBUF + vb * 1040
            S.waitk("scalar", ("vbuf", vb))
            tcs = None
            for dh in range(2):
                S.waitk("tensor", ("pw",))
                tp = None
                for c in range(8):
                    tp = S.op("tensor", lambda E, c=c, dh=dh, s=sub, lo=lnf_off: E.matmul(
                        pW[:, :],
                        AR1[:, lo + c * 512 + s * 128:lo + c * 512 + (s + 1) * 128],
                        AR1[:, WV + c * 1024 + dh * 512:WV + c * 1024 + (dh + 1) * 512],
                        start=(c == 0), stop=(c == 7)), "tP" if c == 7 else None)
                S.wait("scalar", "tP", tp)
                tcs = S.op("scalar", lambda E, dh=dh, vo=voff: E.activation(
                    AR3[:, vo + dh * 520:vo + (dh + 1) * 520]
                    .rearrange("p (h a) -> p h a", a=65)[:, :, 0:64],
                    pW[:, :].rearrange("p (h a) -> p h a", a=64), AF.Copy), "tC")
                last[("pw",)] = ("tC", tcs)
            S.wait("gpsimd", "tC", tcs)
            vg = S.dma("gpsimd", vdst[sub], AR3[:, voff:voff + 1040], "qG")
            last[("vbuf", vb)] = ("qG", vg)
        last[("lnfw", gbuf)] = ("tP", S.ctr["tP"])

    for b in (range(B) if "P" in PHASES else ()):
        for g in range(16):
            proj_512(ap["x"][b, g * 512:(g + 1) * 512, :],
                     [scr["kscr"][b, pr, :, g * 512:(g + 1) * 512] for pr in range(NPAIR)],
                     [scr["vscr"][b, g * 512 + s * 128:g * 512 + (s + 1) * 128, :] for s in range(4)],
                     None)
    for gs in (range(4) if "O" in PHASES else ()):
        proj_512(ap["xown"][gs],
                 [scr["kown"][gs, :, pr * 512:(pr + 1) * 512] for pr in range(NPAIR)],
                 [scr["vown"][gs, s * 128:(s + 1) * 128, :] for s in range(4)],
                 [scr["qscr"][gs, :, pr * 512:(pr + 1) * 512] for pr in range(NPAIR)])
    t_PO_P = S.ctr["tP"]
    v_PO_G = S.ctr["qG"]

    S.wait("sync", "tP", t_PO_P)
    S.wait("vector", "tP", t_PO_P)
    def wo_loader():
        for pr in range(NPAIR):
            S.dma("sync", AR2[:, XB + pr * 1024:XB + (pr + 1) * 1024],
                  ap["wo"][pr * 128:(pr + 1) * 128, :], "qS")
    stage_cast(wo_loader, WO, 8192)
    t_wo_V = S.ctr["tV"]

    S.wait("gpsimd", "qG", v_PO_G)
    wblk = [0]

    def precast(src, dst):
        wb = wblk[0] % 2
        wblk[0] += 1
        S.waitk("gpsimd", ("wstg", wb))
        dst_st = AR2[:, WSTG + wb * 2048:WSTG + (wb + 1) * 2048]
        if len(src.shape) == 3:
            dst_st = dst_st.rearrange("p (s d) -> p s d", d=src.shape[2])
        vq = S.dma("gpsimd", dst_st, src, "qG")
        S.waitk("gpsimd", ("wcst", wb))
        S.wait("gpsimd", "qG", vq)
        tg = S.op("gpsimd", lambda E, wb=wb: E.tensor_copy(
            AR3[:, WCST + wb * 2048:WCST + (wb + 1) * 2048],
            AR2[:, WSTG + wb * 2048:WSTG + (wb + 1) * 2048]), "tG")
        last[("wstg", wb)] = ("tG", tg)
        src_st = AR3[:, WCST + wb * 2048:WCST + (wb + 1) * 2048]
        if len(dst.shape) == 3:
            src_st = src_st.rearrange("p (s d) -> p s d", d=dst.shape[2])
        vg = S.dma("gpsimd", dst, src_st, "qG")
        last[("wcst", wb)] = ("qG", vg)

    for e in (range(E) if "W" in PHASES else ()):
        for blk in range(8):
            for hf in range(2):
                precast(ap["w1"][e, blk * 128:(blk + 1) * 128, hf * 2048:(hf + 1) * 2048],
                        scr["w1bf"][e, blk * 128:(blk + 1) * 128, hf * 2048:(hf + 1) * 2048])
        for blk in range(16):
            precast(ap["w2"][e, blk * 256:(blk + 1) * 256, :].rearrange("(s p) d -> p s d", p=128),
                    scr["w2bf"][e, blk * 256:(blk + 1) * 256, :].rearrange("(s p) d -> p s d", p=128))
        last[("wbf", e)] = ("qG", S.ctr["qG"])
    v_W_G = S.ctr["qG"]
    t_W_G = S.ctr["tG"]

    ffst = {"w1": 0, "w2": 0, "hho": 0}

    def ffn_gen(g):
        for e in range(E):
            S.waitk("sync", ("wbf", e))
            for hh in range(2):
                for oct_ in range(2):
                    hbase = hh * 16 + oct_ * 8
                    hslot = ffst["hho"] % 2
                    ffst["hho"] += 1
                    hoff = HHO + hslot * 4096
                    for hcl in range(8):
                        hc = hbase + hcl
                        wb = ffst["w1"] % 2
                        ffst["w1"] += 1
                        S.waitk("sync", ("w1s", wb))
                        vq = S.dma("sync", AR1[:, W1S + wb * 1024:W1S + (wb + 1) * 1024]
                                   .rearrange("p (c n) -> p c n", n=128),
                                   scr["w1bf"][e, :, hc * 128:(hc + 1) * 128]
                                   .rearrange("(c p) n -> p c n", p=128), "qC")
                        ps1, pkey = cd_slot()
                        S.waitk("tensor", pkey)
                        S.wait("tensor", "qC", vq)
                        tp = None
                        for c in range(8):
                            tp = S.op("tensor", lambda E, c=c, wb=wb, ps1=ps1: E.matmul(
                                ps1[:, :],
                                AR1[:, W1S + wb * 1024 + c * 128:W1S + wb * 1024 + (c + 1) * 128],
                                AR1[:, XN + c * 512:XN + (c + 1) * 512],
                                start=(c == 0), stop=(c == 7)), "tP" if c == 7 else None)
                        last[("w1s", wb)] = ("tP", tp)
                        S.waitk("scalar", ("hho", hslot, hcl))
                        S.wait("scalar", "tP", tp)
                        tc = S.op("scalar", lambda E, hcl=hcl, hc=hc, e=e, ps1=ps1, ho=hoff: E.activation(
                            AR1[:, ho + hcl * 512:ho + (hcl + 1) * 512], ps1[:, :], AF.Gelu,
                            bias=sb["c_b1t"][:, e * 32 + hc:e * 32 + hc + 1]), "tC")
                        last[pkey] = ("tC", tc)
                        yield
                    t_gel = S.ctr["tC"]
                    for dh in range(2):
                        wb2 = ffst["w2"] % 2
                        ffst["w2"] += 1
                        S.waitk("sync", ("w2q", wb2))
                        vq2 = S.dma("sync", AR1[:, W2Q + wb2 * 4096:W2Q + (wb2 + 1) * 4096]
                                    .rearrange("p (k n) -> p k n", n=512),
                                    scr["w2bf"][e, hbase * 128:(hbase + 8) * 128, dh * 512:(dh + 1) * 512]
                                    .rearrange("(k p) n -> p k n", p=128), "qS")
                        for sub in range(4):
                            S.waitk("tensor", ("pw",))
                            S.wait("tensor", "qS", vq2)
                            S.wait("tensor", "tC", t_gel)
                            tp = None
                            for kk in range(8):
                                tp = S.op("tensor", lambda E, kk=kk, sub=sub, wb2=wb2, ho=hoff: E.matmul(
                                    pW[:, :],
                                    AR1[:, ho + kk * 512 + sub * 128:ho + kk * 512 + (sub + 1) * 128],
                                    AR1[:, W2Q + wb2 * 4096 + kk * 512:W2Q + wb2 * 4096 + (kk + 1) * 512],
                                    start=(kk == 0), stop=(kk == 7)), "tP" if kk == 7 else None)
                            last[("w2q", wb2)] = ("tP", tp)
                            soff = SACC + sub * 1024 + dh * 512
                            cwo = CW + sub * 8 + e
                            S.wait("vector", "tP", tp)
                            tv = S.op("vector", lambda E, so=soff, cwo=cwo: E.scalar_tensor_tensor(
                                AR2[:, so:so + 512], pW[:, :], AR2[:, cwo:cwo + 1],
                                AR2[:, so:so + 512], ALU.mult, ALU.add), "tV")
                            last[("pw",)] = ("tV", tv)
                            yield
                    for hcl in range(8):
                        last[("hho", hslot, hcl)] = ("tP", S.ctr["tP"])
        tfin = S.ctr["tV"]
        S.wait("gpsimd", "tV", tfin)
        vg = S.dma("gpsimd", y[g].rearrange("(s p) d -> p s d", p=128),
                   AR2[:, SACC:SACC + 4096].rearrange("p (s d) -> p s d", d=1024), "qG")
        last[("sacc",)] = ("qG", vg)
        last[("xn",)] = ("tP", S.ctr["tP"])
        ffst["done", g] = True
        while True:
            yield

    def ffn_prep(g):
        S.waitk("sync", ("sacc",))
        S.waitk("sync", ("x1s",))
        vx = S.dma("sync", AR2[:, SACC:SACC + 4096].rearrange("p (s d) -> p s d", d=1024),
                   scr["x1scr"][g].rearrange("(s p) d -> p s d", p=128), "qS")
        for sub in range(4):
            soff = SACC + sub * 1024
            S.wait("vector", "qS", vx)
            tln1 = ln_block(AR2[:, soff:soff + 1024], AR2[:, LNA:LNA + 1024],
                            AR3[:, JKF:JKF + 1024])
            S.wait("vector", "tV", tln1)
            tva = S.op("vector", lambda E, so=soff: E.tensor_add(
                AR2[:, so:so + 1024], AR2[:, so:so + 1024], AR2[:, LNA:LNA + 1024]), "tV")
            S.wait("vector", "tV", tva)
            tln2 = ln_block(AR2[:, LNA:LNA + 1024], AR3[:, LN2N:LN2N + 1024],
                            AR3[:, JKF:JKF + 1024])
            S.waitk("tensor", ("pT",))
            S.wait("tensor", "tV", tln2)
            tp = None
            for c in range(8):
                tp = S.op("tensor", lambda E, c=c: E.transpose(
                    pT[:, c * 128:(c + 1) * 128],
                    AR3[:, LN2N + c * 128:LN2N + (c + 1) * 128], c_ident[:, :]),
                    "tP" if c == 7 else None)
            S.wait("vector", "tP", tp)
            tcx = None
            for c in range(8):
                tcx = S.op("vector", lambda E, c=c, sub=sub: E.tensor_scalar(
                    AR1[:, XN + c * 512 + sub * 128:XN + c * 512 + (sub + 1) * 128],
                    pT[:, c * 128:(c + 1) * 128],
                    sb["c_gb"][:, c:c + 1], sb["c_gb"][:, 8 + c:9 + c],
                    ALU.mult, ALU.add), "tV")
            last[("pT",)] = ("tV", tcx)
            S.waitk("tensor", ("pw",))
            S.wait("tensor", "tV", tcx)
            tpg = None
            for c in range(8):
                tpg = S.op("tensor", lambda E, c=c, sub=sub: E.matmul(
                    pW[:, 0:8],
                    AR1[:, XN + c * 512 + sub * 128:XN + c * 512 + (sub + 1) * 128],
                    sb["c_gw"][:, c * 8:(c + 1) * 8],
                    start=(c == 0), stop=(c == 7)), "tP" if c == 7 else None)
            cwo = CW + sub * 8
            sl = stat_slot[0] % 8
            stat_slot[0] += 1
            cm1, cm1n, cm2, cs2 = sl * 8, sl * 8 + 1, sl * 8 + 2, sl * 8 + 3
            S.wait("vector", "tP", tpg)
            w1_ = S.op("vector", lambda E, cm1=cm1: E.reduce_max(
                c_st[:, cm1:cm1 + 1], pW[:, 0:8], axis=AX.X), "tV")
            S.wait("vector", "tV", w1_)
            w2_ = S.op("vector", lambda E, cm1n=cm1n, cm1=cm1: E.tensor_scalar(
                c_st[:, cm1n:cm1n + 1], c_st[:, cm1:cm1 + 1], -1.0, None, ALU.mult), "tV")
            w3_ = S.op("vector", lambda E: E.tensor_copy(
                AR2[:, XOB:XOB + 8], pW[:, 0:8]), "tV")
            S.wait("vector", "tV", w3_)
            w4_ = S.op("vector", lambda E, cm1=cm1: E.tensor_scalar(
                AR2[:, XOB + 8:XOB + 16], AR2[:, XOB:XOB + 8],
                c_st[:, cm1:cm1 + 1], None, ALU.is_equal), "tV")
            S.wait("vector", "tV", w4_)
            w5_ = S.op("vector", lambda E: E.scalar_tensor_tensor(
                AR2[:, XOB + 8:XOB + 16], AR2[:, XOB + 8:XOB + 16], -1e30,
                AR2[:, XOB:XOB + 8], ALU.mult, ALU.add), "tV")
            S.wait("vector", "tV", w5_)
            w6_ = S.op("vector", lambda E, cm2=cm2: E.reduce_max(
                c_st[:, cm2:cm2 + 1], AR2[:, XOB + 8:XOB + 16], axis=AX.X), "tV")
            S.wait("vector", "tV", w6_)
            tvm = S.op("vector", lambda E, cm2=cm2: E.tensor_scalar(
                AR2[:, XOB + 8:XOB + 16], AR2[:, XOB:XOB + 8],
                c_st[:, cm2:cm2 + 1], None, ALU.is_ge), "tV")
            S.wait("scalar", "tV", max(tvm, w2_))
            tce = S.op("scalar", lambda E, cm1n=cm1n: E.activation(
                AR2[:, XOB + 16:XOB + 24], AR2[:, XOB:XOB + 8], AF.Exp,
                bias=c_st[:, cm1n:cm1n + 1]), "tC")
            S.wait("vector", "tC", tce)
            w7_ = S.op("vector", lambda E: E.tensor_mul(
                AR2[:, XOB + 16:XOB + 24], AR2[:, XOB + 16:XOB + 24],
                AR2[:, XOB + 8:XOB + 16]), "tV")
            S.wait("vector", "tV", w7_)
            w8_ = S.op("vector", lambda E, cs2=cs2: E.reduce_sum(
                c_st[:, cs2:cs2 + 1], AR2[:, XOB + 16:XOB + 24], axis=AX.X), "tV")
            S.wait("vector", "tV", w8_)
            w9_ = S.op("vector", lambda E, cs2=cs2: E.reciprocal(
                c_st[:, cs2 + 1:cs2 + 2], c_st[:, cs2:cs2 + 1]), "tV")
            S.wait("vector", "tV", w9_)
            tvc = S.op("vector", lambda E, cwo=cwo, cs2=cs2: E.tensor_scalar(
                AR2[:, cwo:cwo + 8], AR2[:, XOB + 16:XOB + 24],
                c_st[:, cs2 + 1:cs2 + 2], None, ALU.mult), "tV")
            S.wait("vector", "tV", tvc)
            tvb = S.op("vector", lambda E, cwo=cwo: E.tensor_copy(
                AR3[:, CWB:CWB + 8], AR2[:, cwo:cwo + 8]), "tV")
            S.waitk("tensor", ("pcwt",))
            S.wait("tensor", "tV", tvb)
            tpt = S.op("tensor", lambda E: E.transpose(
                pT[0:8, 0:128], AR3[:, CWB:CWB + 8], c_ident[:, :]), "tP")
            S.wait("vector", "tP", tpt)
            tvt = S.op("vector", lambda E, sub=sub: E.tensor_copy(
                sb["c_cwt"][0:8, sub * 128:(sub + 1) * 128], pT[0:8, 0:128]), "tV")
            last[("pcwt",)] = ("tV", tvt)
            S.wait("tensor", "tV", tvt)
            for dh in range(2):
                S.waitk("tensor", ("pw",))
                tpb = S.op("tensor", lambda E, dh=dh, sub=sub: E.matmul(
                    pW[:, :],
                    sb["c_cwt"][0:8, sub * 128:(sub + 1) * 128],
                    sb["c_b2"][0:8, dh * 512:(dh + 1) * 512],
                    start=True, stop=True), "tP")
                S.wait("vector", "tP", tpb)
                tvb2 = S.op("vector", lambda E, dh=dh, so=soff: E.tensor_add(
                    AR2[:, so + dh * 512:so + (dh + 1) * 512],
                    AR2[:, so + dh * 512:so + (dh + 1) * 512], pW[:, :]), "tV")
                last[("pw",)] = ("tV", tvb2)
        last[("xob", 0)] = ("tV", S.ctr["tV"])
        last[("xob", 1)] = ("tV", S.ctr["tV"])

    for k in (("qsl",), ("kvst",), ("xn",)):
        last[k] = ("tP", t_PO_P)
    aabc = [0]
    ofirst = [True] * 16
    pend = {"gen": None}

    def pump(n):
        if pend["gen"] is not None:
            for _ in range(n):
                next(pend["gen"])

    def drain(g):
        while pend["gen"] is not None and not ffst.get(("done", g)):
            next(pend["gen"])

    def attn_group(s, grp, diag):
        for pr in range(NPAIR):
            pump(2)
            ab = aabc[0] % 2
            aabc[0] += 1
            aoff = AAB + ab * 4096
            S.waitk("scalar", ("aab", ab))
            texp = []
            for i in range(GG):
                po = (i % 2) * 1024
                S.waitk("tensor", ("sps", i % 2))
                for h in range(2):
                    inc = (h == 1) and not diag
                    S.op("tensor", lambda E, h=h, i=i, pr=pr, po=po: E.matmul(
                        pS[:, po + h * 512:po + (h + 1) * 512],
                        AR1[h * 64:(h + 1) * 64, KST + pr * 512 + i * 128:KST + pr * 512 + (i + 1) * 128],
                        AR1[h * 64:(h + 1) * 64, QSL + pr * 512:QSL + (pr + 1) * 512],
                        start=True, stop=(not diag), tile_position=(h * 64, 0)),
                        "tP" if inc else None)
                    if diag:
                        S.op("tensor", lambda E, h=h, i=i, po=po: E.matmul(
                            pS[:, po + h * 512:po + (h + 1) * 512], c_ident[:, :],
                            c_maskp[:, i * 512:(i + 1) * 512],
                            start=False, stop=True), "tP" if h == 1 else None)
                tp = S.ctr["tP"]
                S.wait("scalar", "tP", tp)
                bcol = 96 if diag else (grp if s == 0 else 24 + grp)
                tc = S.op("scalar", lambda E, po=po, i=i, ao=aoff, bc=bcol: E.activation(
                    AR1[:, ao + i * 1024:ao + (i + 1) * 1024], pS[:, po:po + 1024], AF.Exp,
                    bias=c_btab[:, bc:bc + 1]), "tC")
                texp.append(tc)
                last[("sps", i % 2)] = ("tC", tc)
            for h in range(2):
                psz, pkey = cd_slot()
                S.waitk("tensor", pkey)
                S.wait("tensor", "tC", texp[-1])
                tp_av = None
                for i in range(GG):
                    tp_av = S.op("tensor", lambda E, i=i, h=h, pr=pr, psz=psz, ao=aoff: E.matmul(
                        psz[0:65, :],
                        AR1[:, VST + i * 1040 + (2 * pr + h) * 65:VST + i * 1040 + (2 * pr + h) * 65 + 65],
                        AR1[:, ao + i * 1024 + h * 512:ao + i * 1024 + (h + 1) * 512],
                        start=(i == 0), stop=(i == GG - 1)), "tP" if i == GG - 1 else None)
                slot = pr * 2 + h
                ooff = OACC + slot * 512
                S.wait("vector", "tP", tp_av)
                if ofirst[slot]:
                    tv = S.op("vector", lambda E, o=ooff, psz=psz: E.tensor_copy(
                        AR2[0:65, o:o + 512], psz[0:65, :]), "tV")
                    ofirst[slot] = False
                else:
                    tv = S.op("vector", lambda E, o=ooff, psz=psz: E.tensor_add(
                        AR2[0:65, o:o + 512], AR2[0:65, o:o + 512], psz[0:65, :]), "tV")
                last[pkey] = ("tV", tv)
            last[("aab", ab)] = ("tP", S.ctr["tP"])

    for bs in (range(4) if "A" in PHASES else ()):
        b, s = bs // 2, bs % 2
        S.waitk("sync", ("qsl",))
        S.wait("sync", "qG", v_PO_G)
        vq = S.dma("sync", AR1[:, QSL:QSL + 4096], scr["qscr"][bs], "qS")
        S.wait("tensor", "qS", vq)
        S.wait("tensor", "tV", t_wo_V)
        for i in range(16):
            ofirst[i] = True
        ngrp = CAP[s] // GG
        for grp in range(ngrp + 1):
            diag = grp == ngrp
            S.waitk("sync", ("kvst",))
            if diag:
                S.dma("sync", AR1[:, KST:KST + 4096], scr["kown"][bs], "qS")
                vv = S.dma("sync", AR1[:, VST:VST + 4160].rearrange("p (i d) -> p i d", d=1040),
                           scr["vown"][bs].rearrange("(i p) d -> p i d", p=128), "qS")
            else:
                S.dma("sync", AR1[:, KST:KST + 4096].rearrange("p (r t) -> p r t", t=512),
                      scr["kscr"][b, :, :, grp * 512:(grp + 1) * 512]
                      .rearrange("r p t -> p r t"), "qS")
                vv = S.dma("sync", AR1[:, VST:VST + 4160].rearrange("p (i d) -> p i d", d=1040),
                           scr["vscr"][b, grp * 512:(grp + 1) * 512, :]
                           .rearrange("(i p) d -> p i d", p=128), "qS")
            S.wait("tensor", "qS", vv)
            attn_group(s, grp, diag)
            last[("kvst",)] = ("tP", S.ctr["tP"])
            pump(2 if s == 0 else 14)

        for pr in range(NPAIR):
            for h in range(2):
                slot = pr * 2 + h
                ooff = OACC + slot * 512
                S.waitk("scalar", ("zrw",))
                z1 = S.op("scalar", lambda E, o=ooff: E.activation(
                    AR2[0:1, LNA:LNA + 512], AR2[64:65, o:o + 512], AF.Ln), "tC")
                S.wait("scalar", "tC", z1)
                z1b = S.op("scalar", lambda E: E.activation(
                    AR2[0:1, LNA:LNA + 512], AR2[0:1, LNA:LNA + 512], AF.Exp,
                    scale=c_btab[0:1, 95:96]), "tC")
                S.wait("vector", "tC", z1b)
                z2 = S.op("vector", lambda E: E.tensor_copy(
                    AR3[0:1, JKF:JKF + 512], AR2[0:1, LNA:LNA + 512]), "tV")
                S.wait("vector", "tV", z2)
                z3 = S.op("vector", lambda E: E.tensor_sub(
                    AR2[0:1, LNA:LNA + 512], AR2[0:1, LNA:LNA + 512], AR3[0:1, JKF:JKF + 512]), "tV")
                S.wait("vector", "tV", z3)
                tvr = S.op("vector", lambda E: E.tensor_copy(
                    AR3[0:1, JKF + 512:JKF + 1024], AR2[0:1, LNA:LNA + 512]), "tV")
                psz, pkey = cd_slot()
                S.waitk("tensor", pkey)
                S.wait("tensor", "tV", tvr)
                S.op("tensor", lambda E, psz=psz: E.matmul(
                    psz[0:64, :], sb["c_onb"][0:1, :], AR3[0:1, JKF:JKF + 512],
                    start=True, stop=False), None)
                tpb = S.op("tensor", lambda E, psz=psz: E.matmul(
                    psz[0:64, :], sb["c_onb"][0:1, :], AR3[0:1, JKF + 512:JKF + 1024],
                    start=False, stop=True), "tP")
                last[("zrw",)] = ("tP", tpb)
                S.wait("vector", "tP", tpb)
                tvn = S.op("vector", lambda E, o=ooff, h=h, pr=pr, psz=psz: E.tensor_mul(
                    AR1[h * 64:(h + 1) * 64, ONORM + pr * 512:ONORM + (pr + 1) * 512],
                    AR2[0:64, o:o + 512], psz[0:64, :]), "tV")
                last[pkey] = ("tV", tvn)
            pump(6)
        v_on = S.ctr["tV"]

        for sub in range(4):
            xb2 = sub % 2
            S.waitk("sync", ("xob", xb2))
            vx = S.dma("sync", AR2[:, XOB + xb2 * 1024:XOB + (xb2 + 1) * 1024],
                       ap["xown"][bs, sub * 128:(sub + 1) * 128, :], "qS")
            tvs = None
            for dh in range(2):
                S.waitk("tensor", ("pw",))
                S.wait("tensor", "tV", v_on)
                tp = None
                for pr in range(NPAIR):
                    tp = S.op("tensor", lambda E, dh=dh, pr=pr, sub=sub: E.matmul(
                        pW[:, :],
                        AR1[:, ONORM + pr * 512 + sub * 128:ONORM + pr * 512 + (sub + 1) * 128],
                        AR1[:, WO + pr * 1024 + dh * 512:WO + pr * 1024 + (dh + 1) * 512],
                        start=(pr == 0), stop=(pr == 7)), "tP" if pr == 7 else None)
                x1b = sub % 2
                S.waitk("vector", ("x1b", x1b))
                S.wait("vector", "tP", tp)
                S.wait("vector", "qS", vx)
                tvs = S.op("vector", lambda E, dh=dh, x1b=x1b, xb2=xb2: E.tensor_add(
                    AR2[:, X1B + x1b * 1024 + dh * 512:X1B + x1b * 1024 + (dh + 1) * 512],
                    pW[:, :],
                    AR2[:, XOB + xb2 * 1024 + dh * 512:XOB + xb2 * 1024 + (dh + 1) * 512]), "tV")
                last[("pw",)] = ("tV", tvs)
            last[("xob", xb2)] = ("tV", tvs)
            S.wait("scalar", "tV", tvs)
            vg = S.dma("scalar", scr["x1scr"][bs, sub * 128:(sub + 1) * 128, :],
                       AR2[:, X1B + (sub % 2) * 1024:X1B + (sub % 2 + 1) * 1024], "qC")
            last[("x1b", sub % 2)] = ("qC", vg)
            pump(6)
        last[("qsl",)] = ("tP", S.ctr["tP"])
        last[("kvst",)] = ("tP", S.ctr["tP"])
        last[("x1s",)] = ("qC", S.ctr["qC"])

        if "F" in PHASES:
            if pend["gen"] is not None:
                drain(bs - 1)
            S.wait("sync", "qC", S.ctr["qC"])
            ffn_prep(bs)
            pend["gen"] = ffn_gen(bs)
    if "F" in PHASES and pend["gen"] is not None:
        drain(3)

    if taps:
        S.wait("gpsimd", "qG", S.ctr["qG"])
        S.wait("gpsimd", "qC", S.ctr["qC"])
        S.wait("gpsimd", "tV", S.ctr["tV"])
        S.dma("gpsimd", taps["t_kscr"], scr["kscr"][0], "qG")
        S.dma("gpsimd", taps["t_vscr"], scr["vscr"][0], "qG")
        S.dma("gpsimd", taps["t_qscr"], scr["qscr"][:, :, :], "qG")
        S.dma("gpsimd", taps["t_x1"], scr["x1scr"][:, :, :], "qG")
        S.dma("gpsimd", taps["t_w1bf"], scr["w1bf"][0], "qG")
        S.dma("gpsimd", taps["t_w2bf"], scr["w2bf"][0], "qG")
    S.wait("gpsimd", "qG", S.ctr["qG"])


def _host_constants(j):
    ident = np.eye(128, dtype=np.float32)
    maskp = np.zeros((4, 128, 512), np.float32)
    for i in range(4):
        kt = i * 128 + np.arange(128)[:, None]
        q = np.arange(512)[None, :]
        maskp[i] = np.where(kt <= q, 0.0, NEGBIG)
    btab = np.zeros((128, 100), np.float32)
    len0 = 4 * (16 - j)
    len1 = 4 * (j + 1)
    for gg in range(16):
        btab[:, gg] = -SHIFT if gg * 4 < len0 - 4 else NEGBIG
    for gg in range(8):
        btab[:, 24 + gg] = -SHIFT if gg * 4 < len1 - 4 else NEGBIG
    btab[:, 95] = -1.0
    btab[:, 96] = -SHIFT
    btab[:, 97] = 1e-5
    btab[:, 98] = -0.5
    btab[:, 99] = 0.125
    return ident, maskp, btab


def kernel(L0_attn_x, L0_attn_w_q, L0_attn_w_k, L0_attn_w_v, L0_attn_w_o,
           L0_ffn_gate_w, L0_ffn_w1, L0_ffn_b1, L0_ffn_w2, L0_ffn_b2,
           L0_ffn_gamma, L0_ffn_beta):
    x = np.ascontiguousarray(np.asarray(L0_attn_x, np.float32))
    shared = {
        "x": x,
        "wq": np.ascontiguousarray(np.asarray(L0_attn_w_q, np.float32)),
        "wk": np.ascontiguousarray(np.asarray(L0_attn_w_k, np.float32)),
        "wv": np.ascontiguousarray(np.asarray(L0_attn_w_v, np.float32)),
        "wo": np.ascontiguousarray(np.asarray(L0_attn_w_o, np.float32)),
        "gw": np.ascontiguousarray(np.asarray(L0_ffn_gate_w, np.float32)),
        "w1": np.ascontiguousarray(np.asarray(L0_ffn_w1, np.float32)),
        "b1": np.ascontiguousarray(np.asarray(L0_ffn_b1, np.float32)),
        "gbc": np.ascontiguousarray(np.concatenate(
            [np.asarray(L0_ffn_gamma, np.float32).reshape(8, 128).T,
             np.asarray(L0_ffn_beta, np.float32).reshape(8, 128).T], axis=1)),
        "b1t": np.ascontiguousarray(
            np.asarray(L0_ffn_b1, np.float32).reshape(E * 32, 128).T),
        "w2": np.ascontiguousarray(np.asarray(L0_ffn_w2, np.float32)),
        "b2": np.ascontiguousarray(np.asarray(L0_ffn_b2, np.float32)),
    }
    in_maps = []
    for j in range(NCORE):
        ident, maskp, btab = _host_constants(j)
        rows = (15 - j, j)
        xo = np.stack([x[b, r * 512:(r + 1) * 512, :] for b in range(B) for r in rows])
        m = dict(shared)
        m["xown"] = np.ascontiguousarray(xo)
        m["ident"] = ident
        m["maskp"] = np.ascontiguousarray(maskp)
        m["btab"] = btab
        in_maps.append(m)

    nc = build_program()
    res = run_bass_kernel_spmd(nc, in_maps, core_ids=list(range(NCORE)))
    out = np.empty((B, T, D), np.float32)
    for j in range(NCORE):
        yj = np.asarray(res.results[j]["y"], np.float32)
        rows = (15 - j, j)
        for b in range(B):
            for si, r in enumerate(rows):
                out[b, r * 512:(r + 1) * 512, :] = yj[b * 2 + si]
    return out
